# version 07
# baseline (speedup 1.0000x reference)
import numpy as np
import ml_dtypes
import concourse.bass as bass
import concourse.mybir as mybir
from concourse.bass_utils import run_bass_kernel_spmd

F32 = mybir.dt.float32
BF16 = mybir.dt.bfloat16
AF = mybir.ActivationFunctionType
ALU = mybir.AluOpType
AX = mybir.AxisListType

T = 2048
D = 1024
NDC = 8
H = 16
DH = 64
NE = 8
FE = 512
FF = 4096
NTC = 16
NTQ = 4
WIDE = 3968
LN_EPS = 1e-5
JUNK = 1

STAGE = 3
DEBUG = False

ENGINES = ("sync", "tensor", "vector", "scalar", "gpsimd")
DMA_SEM = {"sync": "ds", "gpsimd": "dg", "scalar": "dsc"}


class Kern:
    def __init__(self):
        self.nc = bass.Bass("TRN2", target_bir_lowering=False)
        self.steps = []
        self.ctr = {}
        self.dbg_names = []

    def dbg(self, ap, wait=()):
        if not DEBUG:
            return None
        name = "dbg%d" % len(self.dbg_names)
        d = self.nc.dram_tensor(name, list(ap.shape), ap.dtype,
                                kind="ExternalOutput").ap()
        self.dbg_names.append(name)
        return self.dma("sync", d, ap, wait=wait)

    def on(self, eng, fn, wait=(), inc=None, inc_amt=1):
        waits = tuple(w for w in wait if w is not None)
        tok = None
        if inc is not None:
            self.ctr[inc] = self.ctr.get(inc, 0) + inc_amt
            tok = (inc, self.ctr[inc])
        self.steps.append((eng, fn, waits, (inc, inc_amt) if inc else None))
        return tok

    def dma(self, eng, out, in_, wait=(), slow=False):
        return self.on(eng, lambda e, o=out, i=in_: e.dma_start(
            o, i, allow_slow_non_contiguous=slow),
            wait=wait, inc=DMA_SEM[eng], inc_amt=16)

    def emit(self, block, sems):
        for s in ("ds", "dg", "dsc"):
            if self.ctr.get(s, 0) > 0:
                v = self.ctr[s]
                self.steps.append(
                    ("sync", lambda e: e.nop(), ((s, v),), None))
        seen = {e: {} for e in ENGINES}

        def mk(eng_name):
            def body(eng):
                for (e, fn, waits, inc) in self.steps:
                    if e != eng_name:
                        continue
                    for (s, v) in waits:
                        if seen[eng_name].get(s, 0) < v:
                            eng.wait_ge(sems[s], v)
                            seen[eng_name][s] = v
                    ins = fn(eng)
                    if inc is not None:
                        ins.then_inc(sems[inc[0]], inc[1])
            return body

        for name in ENGINES:
            getattr(block, name)(mk(name))

    def declare(self):
        nc = self.nc

        def din(name, shape, dtype=F32):
            return nc.dram_tensor(name, shape, dtype, kind="ExternalInput").ap()

        self.x0_d = din("x0", [T, D])
        self.Wq_d = din("Wq", [D, D]); self.bq_d = din("bq", [1, D])
        self.Wk_d = din("Wk", [D, D]); self.bk_d = din("bk", [1, D])
        self.Wv_d = din("Wv", [D, D]); self.bv_d = din("bv", [1, D])
        self.Wo_d = din("Wo", [D, D]); self.bo_d = din("bo", [1, D])
        self.ds_d = din("dsc0", [1, 1])
        self.W1a_d = din("W1a", [D, FF]); self.b1a_d = din("b1a", [1, FF])
        self.W2a_d = din("W2a", [FF, D]); self.b2a_d = din("b2a", [1, D])
        self.g1_d = din("g1", [1, D]); self.be1_d = din("be1", [1, D])
        self.g2_d = din("g2", [1, D]); self.be2_d = din("be2", [1, D])
        self.Ws1_d = din("Ws1", [D, FE]); self.Ws2_d = din("Ws2", [FE, D])
        self.W1e_d = din("W1e", [NE, D, FE]); self.W2e_d = din("W2e", [NE, FE, D])
        self.Wr_d = din("Wr", [D, NE])
        self.identf_d = din("identf", [128, 128])
        self.onesr_d = din("onesr", [1, 128])
        self.onesrb_d = din("onesrb", [1, 128], BF16)
        self.onescb_d = din("onescb", [128, 1], BF16)
        self.widea_d = din("widea", [128, WIDE])

        self.wc_W1a_d = nc.dram_tensor("wcW1a", [D, FF], BF16, kind="Internal").ap()
        self.wc_W2a_d = nc.dram_tensor("wcW2a", [FF, D], BF16, kind="Internal").ap()
        self.wc_W1e_d = nc.dram_tensor("wcW1e", [NE, D, FE], BF16, kind="Internal").ap()
        self.wc_W2e_d = nc.dram_tensor("wcW2e", [NE, FE, D], BF16, kind="Internal").ap()
        self.wc_Ws1_d = nc.dram_tensor("wcWs1", [D, FE], BF16, kind="Internal").ap()
        self.wc_Ws2_d = nc.dram_tensor("wcWs2", [FE, D], BF16, kind="Internal").ap()
        self.out_d = nc.dram_tensor("out", [T, D], F32, kind="ExternalOutput").ap()
        self.x1sp_d = nc.dram_tensor("x1sp", [128, NDC * T], F32, kind="Internal").ap()
        self.rowsp_d = nc.dram_tensor("rowsp", [1, 2 * T], F32, kind="Internal").ap()
        self.probsp_d = nc.dram_tensor("probsp", [NE, T], BF16, kind="Internal").ap()

    def views(self):
        self.xr = self.AR1[:, :].bitcast(F32)
        self.pAq = [self.pA[:, 512 * i:512 * (i + 1)] for i in range(4)]
        self.pBq = [self.pB[:, 512 * i:512 * (i + 1)] for i in range(2)]
        self.ys_ring = [self.pBq[0], self.pBq[1], self.pC, self.pD]

        sm = self.AR7[:, 4096:4608].bitcast(F32)
        self.sm = sm
        self.lt = sm[:, 0:8]
        self.v1c = sm[:, 8:9]; self.v2c = sm[:, 9:10]; self.dcol = sm[:, 10:11]
        self.edc = sm[:, 11:12]; self.denc = sm[:, 12:13]
        self.rw1c = sm[:, 13:14]; self.rw2c = sm[:, 14:15]
        self.p1fc = sm[:, 15:16]; self.p2fc = sm[:, 16:17]
        self.eq1m = sm[:, 17:25]; self.ltmm = sm[:, 25:33]
        self.eq2m = sm[:, 33:41]; self.t2m = sm[:, 41:49]
        self.probm = sm[:, 49:57]
        self.bq8c = sm[:, 57:65]
        self.row_slot = self.AR7[0:1, 0:4096].bitcast(F32)
        self.prob_fm = self.AR7[0:8, 0:2048]

    def phase_in(self):
        p = self
        in_ring = [self.AR6[:, 0:2048].bitcast(F32),
                   self.AR6[:, 2048:4096].bitcast(F32)]
        tok_dma = [None] * NTC
        tok_done = [None] * NTC
        tr_evac = {}
        last_tr = None
        for tc in range(NTC):
            slot = in_ring[tc % 2]
            w = [tok_done[tc - 2]] if tc >= 2 else []
            tok_dma[tc] = p.dma("sync", slot,
                                self.x0_d[128 * tc:128 * (tc + 1), :], wait=w)
            last_v = None
            for dc in range(NDC):
                widx = tc * NDC + dc
                ps = self.pD[:, 0:128] if widx % 2 == 0 else self.pC[:, 0:128]
                last_tr = p.on(
                    "tensor",
                    lambda e, ps=ps, slot=slot, dc=dc: e.transpose(
                        ps, slot[:, 128 * dc:128 * (dc + 1)], self.identf[:, :]),
                    wait=[tok_dma[tc], self.tok_ident,
                          tr_evac.get(widx - 2)],
                    inc="t")
                last_v = p.on(
                    "vector",
                    lambda e, ps=ps, dc=dc, tc=tc: e.tensor_copy(
                        self.xr[:, 2048 * dc + 128 * tc:2048 * dc + 128 * (tc + 1)],
                        ps),
                    wait=[last_tr], inc="v")
                tr_evac[widx] = last_v
            tok_done[tc] = last_v
        self.tok_x_ready = tok_done[-1]
        self.tok_in_trans_done = last_tr

    def ln(self, X, OUT, tok_x, mub, rb, xbf_ring, tsc_ring,
           gamma_col=None, beta_col=None, spill=False, extra_wait=(),
           psum_guard=(), apply_wait=(), row_wait=()):
        p = self
        tok_cast = [None] * NDC
        tok_sq = [None] * NDC
        tok_mm = [None] * NDC
        for dc in range(NDC):
            xb = xbf_ring[dc % 2]
            w = [tok_x] + list(extra_wait) + ([tok_mm[dc - 2]] if dc >= 2 else [])
            tok_cast[dc] = p.on("scalar",
                                lambda e, xb=xb, dc=dc: e.copy(xb, X(dc)),
                                wait=w, inc="s")
            sq = self.sq_ring[dc % 2]
            tok_sq[dc] = p.on("vector",
                              lambda e, xb=xb, sq=sq: e.tensor_mul(sq, xb, xb),
                              wait=[tok_cast[dc]], inc="v")
            last = None
            for q in range(4):
                p.on("tensor",
                     lambda e, xb=xb, q=q, dc=dc: e.matmul(
                         self.pAq[q][0:1, :], self.onescb[:, :],
                         xb[:, 512 * q:512 * (q + 1)],
                         start=(dc == 0), stop=(dc == NDC - 1)),
                     wait=[tok_cast[dc]] + (list(psum_guard) if dc == 0 else []),
                     inc=None)
            s2ps = [self.pBq[0], self.pBq[1], self.pC, self.pD]
            for q in range(4):
                last = p.on("tensor",
                            lambda e, sq=sq, q=q, dc=dc, s2=s2ps[q]: e.matmul(
                                s2[0:1, :], self.onescb[:, :],
                                sq[:, 512 * q:512 * (q + 1)],
                                start=(dc == 0), stop=(dc == NDC - 1)),
                            wait=[tok_sq[dc]],
                            inc="t" if q == 3 else None)
            tok_mm[dc] = last
        tok_sums = tok_mm[NDC - 1]
        row = self.row_slot
        t1 = p.on("vector", lambda e: e.tensor_copy(
            row, self.pA[0:1, 0:2048]), wait=[tok_sums] + list(row_wait), inc="v")
        t2 = p.on("vector", lambda e: e.tensor_scalar_mul(row, row, 1.0 / D),
                  wait=[t1], inc="v")
        tok_sp_mu = None
        if spill:
            tok_sp_mu = p.dma("sync", self.rowsp_d[:, 0:T], row, wait=[t2])
        tok_bc = []
        for q in range(4):
            tb = p.on("tensor", lambda e, q=q: e.matmul(
                self.pAq[q], self.onesr[:, :], row[:, 512 * q:512 * (q + 1)],
                start=True, stop=True), wait=[t2], inc="t")
            te = p.on("scalar", lambda e, q=q, mub=mub: e.copy(
                mub[:, 512 * q:512 * (q + 1)], self.pAq[q]),
                wait=[tb], inc="s")
            tok_bc.append(te)
        t3 = p.on("vector", lambda e: e.tensor_mul(
            self.pA[0:1, 0:2048], row, row), wait=[t2] + tok_bc, inc="v")
        t4 = p.on("vector", lambda e: e.tensor_copy(row[:, 0:512], self.pBq[0][0:1, :]),
                  wait=[t3, tok_sp_mu], inc="v")
        t4b = p.on("vector", lambda e: e.tensor_copy(row[:, 512:1024], self.pBq[1][0:1, :]),
                   wait=[t4], inc="v")
        t4c = p.on("vector", lambda e: e.tensor_copy(row[:, 1024:1536], self.pC[0:1, :]),
                   wait=[t4b], inc="v")
        t4d = p.on("vector", lambda e: e.tensor_copy(row[:, 1536:2048], self.pD[0:1, :]),
                   wait=[t4c], inc="v")
        t5 = p.on("vector", lambda e: e.scalar_tensor_tensor(
            row, row, 1.0 / D, self.pA[0:1, 0:2048], ALU.mult, ALU.subtract),
            wait=[t4d], inc="v")
        t6 = p.on("scalar", lambda e: e.activation(row, row, AF.Ln, bias=LN_EPS, scale=1.0),
                  wait=[t5], inc="s")
        t7 = p.on("scalar", lambda e: e.activation(row, row, AF.Exp, bias=0.0, scale=-0.5),
                  wait=[t6], inc="s")
        tok_sp_r = None
        if spill:
            tok_sp_r = p.dma("sync", self.rowsp_d[:, T:2 * T], row, wait=[t7])
        tok_bc2 = []
        for q in range(4):
            tb = p.on("tensor", lambda e, q=q: e.matmul(
                self.pAq[q], self.onesr[:, :], row[:, 512 * q:512 * (q + 1)],
                start=True, stop=True), wait=[t7, t5], inc="t")
            te = p.on("scalar", lambda e, q=q, rb=rb: e.copy(
                rb[:, 512 * q:512 * (q + 1)], self.pAq[q]),
                wait=[tb], inc="s")
            tok_bc2.append(te)
        tok_tiles = tok_bc + tok_bc2 + [tok_sp_mu, tok_sp_r]
        tok_out = []
        last = None
        tg = None
        for dc in range(NDC):
            ts = tsc_ring[dc % 2]
            tg = p.on("gpsimd", lambda e, ts=ts, dc=dc, mub=mub: e.tensor_sub(
                ts, X(dc), mub),
                wait=tok_tiles + list(apply_wait) +
                     [tok_out[-2] if len(tok_out) >= 2 else None],
                inc="g")
            if gamma_col is None:
                tv = p.on("vector", lambda e, ts=ts, dc=dc, rb=rb: e.tensor_mul(
                    OUT(dc), ts, rb), wait=[tg], inc="v")
            else:
                u = ts
                tv0 = p.on("vector", lambda e, ts=ts, rb=rb: e.tensor_mul(
                    ts, ts, rb), wait=[tg], inc="v")
                tv = p.on("scalar", lambda e, ts=ts, dc=dc: e.activation(
                    OUT(dc), ts, AF.Identity, bias=beta_col(dc), scale=gamma_col(dc)),
                    wait=[tv0], inc="s")
            tok_out.append(tv)
            last = tv
        return last, tok_out, tg

    def moe(self, xn, tok_xn, tok_xn_all, tok_arena_free=(), tag=0,
            tok_xres=None, cached=False, tok_cache=None):
        p = self
        nc = self.nc
        w1v = [self.AR4[:, 0:4096].rearrange("p (c f) -> p c f", c=8),
               self.AR4[:, 4096:8192].rearrange("p (c f) -> p c f", c=8)]
        w2v = [self.AR4[:, 8192:12288].rearrange("p (c f) -> p c f", c=4),
               self.AR4[:, 12288:16384].rearrange("p (c f) -> p c f", c=4)]
        pb_slot = [self.AR6[:, 0:2048], self.AR6[:, 2048:4096]]
        rp_scr = self.AR6[:, 4096 + 512 * 0:4096 + 512 * 1]
        rp_scr2 = self.AR6[:, 4608:5120]
        rps = [rp_scr, rp_scr2]

        h2 = self.AR3[:, 0:16384]

        def h2s(bi, fc, tq):
            off = bi * 8192 + fc * 2048 + tq * 512
            return h2[:, off:off + 512]

        groups = [("s", 0), (1, 2), (3, 4), (5, 6), (7,)]

        weng = "sync" if cached else "gpsimd"

        def w1_dram(br):
            if cached:
                src = self.wc_Ws1_d if br == "s" else self.wc_W1e_d[br]
            else:
                src = self.Ws1_d if br == "s" else self.W1e_d[br]
            return src.rearrange("(c p) f -> p c f", p=128)

        def w2_dram(br):
            if cached:
                src = self.wc_Ws2_d if br == "s" else self.wc_W2e_d[br]
            else:
                src = self.Ws2_d if br == "s" else self.W2e_d[br]
            return src.rearrange("(c p) f -> p c f", p=128)

        tok_lg_evac = None
        tok_tr_evac = None
        tok_probm_free = None
        tok_probs = []
        for tc in range(NTC):
            last = None
            for dc in range(NDC):
                last = p.on("tensor", lambda e, dc=dc, tc=tc: e.matmul(
                    self.pD[:, 0:8],
                    xn(dc)[:, 128 * tc:128 * (tc + 1)],
                    self.wrsb[:, :].rearrange("p (c f) -> p c f", c=8)[:, dc, :],
                    start=(dc == 0), stop=(dc == NDC - 1)),
                    wait=([tok_xn_all, self.tok_wr, tok_lg_evac] if dc == 0 else []),
                    inc="t" if dc == NDC - 1 else None)
            tlg = last
            V = "vector"
            prev = [None]

            def vstep(fn, extra=()):
                prev[0] = p.on(V, fn, wait=[prev[0]] + list(extra), inc="v")
                return prev[0]

            t1 = vstep(lambda e: e.tensor_copy(self.lt, self.pD[:, 0:8]),
                       extra=[tlg])
            tok_lg_evac = t1
            vstep(lambda e: e.reduce_max(self.v1c, self.lt, axis=AX.X))
            vstep(lambda e: e.tensor_single_scalar(self.eq1m, self.lt, self.v1c, ALU.is_ge))
            vstep(lambda e: e.scalar_tensor_tensor(
                self.ltmm, self.eq1m, -1e30, self.lt, ALU.mult, ALU.add))
            vstep(lambda e: e.reduce_max(self.v2c, self.ltmm, axis=AX.X))
            tv = vstep(lambda e: e.tensor_sub(self.dcol, self.v2c, self.v1c))
            te = p.on("scalar", lambda e: e.activation(self.edc, self.dcol, AF.Exp),
                      wait=[tv], inc="s")
            vstep(lambda e: e.tensor_single_scalar(self.denc, self.edc, 1.0, ALU.add),
                  extra=[te])
            vstep(lambda e: e.reciprocal(self.rw1c, self.denc))
            vstep(lambda e: e.tensor_mul(self.rw2c, self.edc, self.rw1c))
            vstep(lambda e: e.tensor_single_scalar(self.p1fc, self.rw1c, 1.0, ALU.is_ge))
            vstep(lambda e: e.tensor_single_scalar(self.p2fc, self.rw2c, 1.0, ALU.is_ge))
            vstep(lambda e: e.tensor_single_scalar(self.eq2m, self.ltmm, self.v2c, ALU.is_ge))
            vstep(lambda e: e.tensor_single_scalar(self.t2m, self.eq2m, self.p2fc, ALU.mult))
            tpr = vstep(lambda e: e.scalar_tensor_tensor(
                self.probm, self.eq1m, self.p1fc, self.t2m, ALU.mult, ALU.add),
                extra=[tok_probm_free])
            ttr = p.on("tensor", lambda e: e.transpose(
                self.pC[0:8, 0:128], self.probm, self.identf[:, :]),
                wait=[tpr, tok_tr_evac], inc="t")
            tok_probm_free = ttr
            tev = p.on("vector", lambda e, tc=tc: e.tensor_copy(
                self.prob_fm[:, 128 * tc:128 * (tc + 1)], self.pC[0:8, 0:128]),
                wait=[ttr], inc="v")
            tok_tr_evac = tev
            tok_probs.append(tev)
        tok_prob_sp = p.dma("sync", self.probsp_d, self.prob_fm, wait=[tok_probs[-1]])
        self.dbg(self.prob_fm, wait=[tok_probs[-1]])

        tok_w1 = {}
        tok_w2 = {}
        tok_he_T = {}
        tok_ys_T = {}
        tok_h2 = {}
        tok_pb = {}
        tok_gp_last = {}
        rp_free = [None, None]
        tok_x_upd = None
        tok_x_tq = [None] * NTQ
        un = [0]
        yn = [0]
        prev_pA_free = {}
        ys_free = [None] * 4

        for gi, g in enumerate(groups):
            for j, br in enumerate(g):
                w = [tok_he_T.get(gi - 1)]
                if gi == 0:
                    w = list(tok_arena_free)
                tok_w1[(gi, j)] = p.dma(
                    weng, w1v[j][:, :, :], w1_dram(br),
                    wait=list(w) + [tok_cache])
            for j, br in enumerate(g):
                if br != "s":
                    tok_pb[(gi, j)] = p.dma(
                        "sync", pb_slot[j],
                        self.probsp_d[br:br + 1, :].broadcast_to([128, T]),
                        wait=[tok_prob_sp, tok_gp_last.get(gi - 1)])
            for j, br in enumerate(g):
                for tq in range(NTQ):
                    for fcp in range(2):
                        u = un[0]; un[0] += 1
                        pp = [self.pAq[2 * (u % 2)], self.pAq[2 * (u % 2) + 1]]
                        lastmm = None
                        for dc in range(NDC):
                            for f2 in range(2):
                                fc = fcp * 2 + f2
                                lastmm = p.on("tensor", lambda e, j=j, dc=dc, fc=fc, tq=tq, ps=pp[f2]: e.matmul(
                                    ps, w1v[j][:, dc, 128 * fc:128 * (fc + 1)],
                                    xn(dc)[:, 512 * tq:512 * (tq + 1)],
                                    start=(dc == 0), stop=(dc == NDC - 1)),
                                    wait=([tok_w1[(gi, j)], tok_xn_all,
                                           prev_pA_free.get(u % 2)] if dc == 0 and f2 == 0 else []),
                                    inc="t" if (dc == NDC - 1 and f2 == 1) else None)
                        tok_he_T[gi] = lastmm
                        last_ev = None
                        for f2 in range(2):
                            fc = fcp * 2 + f2
                            re = h2s(j, fc, tq)
                            tre = p.on("scalar", lambda e, re=re, ps=pp[f2]: e.activation(
                                re, ps, AF.Relu), wait=[lastmm], inc="s")
                            last_ev = tre
                            if br == "s":
                                th = p.on("vector", lambda e, re=re: e.tensor_mul(
                                    re, re, re), wait=[tre], inc="v")
                            else:
                                rp = rps[f2]
                                tg_ = p.on("vector", lambda e, re=re, rp=rp, j=j, tq=tq: e.tensor_mul(
                                    rp, re, pb_slot[j][:, 512 * tq:512 * (tq + 1)]),
                                    wait=[tre, tok_pb[(gi, j)], rp_free[f2]], inc="v")
                                tok_gp_last[gi] = tg_
                                th = p.on("vector", lambda e, re=re, rp=rp: e.tensor_mul(
                                    re, rp, re), wait=[tg_], inc="v")
                                rp_free[f2] = th
                            tok_h2[(gi, tq)] = th
                        prev_pA_free[u % 2] = last_ev
            for j, br in enumerate(g):
                tok_w2[(gi, j)] = p.dma(
                    weng, w2v[j][:, :, :], w2_dram(br),
                    wait=[tok_ys_T.get(gi - 1), tok_cache] +
                         (list(tok_arena_free) if gi == 0 else []))
            for tq in range(NTQ):
                for do in range(NDC):
                    k = yn[0] % 4; yn[0] += 1
                    ps = self.ys_ring[k]
                    lastmm = None
                    nbr = len(g)
                    for j in range(nbr):
                        for fc in range(4):
                            first = (j == 0 and fc == 0)
                            lastmm = p.on("tensor", lambda e, j=j, fc=fc, do=do, tq=tq, ps=ps: e.matmul(
                                ps, w2v[j][:, fc, 128 * do:128 * (do + 1)],
                                h2s(j, fc, tq),
                                start=(j == 0 and fc == 0), stop=(j == nbr - 1 and fc == 3)),
                                wait=([tok_w2[(gi, 0)], tok_w2[(gi, nbr - 1)],
                                       tok_h2[(gi, tq)], ys_free[k]] if first else []),
                                inc="t" if (j == nbr - 1 and fc == 3) else None)
                    tok_ys_T[gi] = lastmm
                    tok_x_upd = p.on("vector", lambda e, do=do, tq=tq, ps=ps: e.scalar_tensor_tensor(
                        self.xr[:, 2048 * do + 512 * tq:2048 * do + 512 * (tq + 1)],
                        ps, 0.5,
                        self.xr[:, 2048 * do + 512 * tq:2048 * do + 512 * (tq + 1)],
                        ALU.mult, ALU.add),
                        wait=[lastmm, tok_xres], inc="v")
                    ys_free[k] = tok_x_upd
                    if gi == len(groups) - 1 and do == NDC - 1:
                        tok_x_tq[tq] = tok_x_upd
        return tok_x_upd, tok_ys_T[len(groups) - 1], tok_x_tq

    def qkv_phase(self, xn, tok_xn_all, tok_x_spill, tok_gp_done):
        p = self
        wslot = [self.AR4[:, 0:8192].rearrange("p (c f) -> p c f", c=8),
                 self.AR4[:, 8192:16384].rearrange("p (c f) -> p c f", c=8)]
        qv = self.AR1[:, 0:16384]
        kv = self.AR1[:, 16384:32768]
        tok_bq8 = p.on("vector", lambda e: e.tensor_scalar_mul(
            self.bq8c, self.bq8c, 0.125), wait=[self.tok_bq], inc="v")
        bvb = self.AR5[:, 0:2048].bitcast(F32)
        tok_bvb = p.dma("sync", bvb, self.bv_d.broadcast_to([128, D]),
                        wait=[self.tok_bvb_free])
        tok_wq = p.dma("gpsimd", wslot[0][:, :, :],
                       self.Wq_d.rearrange("(c p) f -> p c f", p=128),
                       wait=[self.tok_w_arena_free])
        tok_wk = p.dma("gpsimd", wslot[1][:, :, :],
                       self.Wk_d.rearrange("(c p) f -> p c f", p=128),
                       wait=[self.tok_w_arena_free])
        ring = [self.pAq[0], self.pAq[1], self.pAq[2], self.pAq[3]]
        rfree = [self.tok_psum_free] * 4
        un = [0]

        def proj(wv_, tok_w, dest, bias_col, scale, extra_w=()):
            toks = []
            for do in range(NDC):
                for tp in range(NTQ):
                    k = un[0] % 4; un[0] += 1
                    ps = ring[k]
                    lastmm = None
                    for dc in range(NDC):
                        lastmm = p.on("tensor", lambda e, dc=dc, do=do, tp=tp, ps=ps, wv_=wv_: e.matmul(
                            ps, wv_[:, dc, 128 * do:128 * (do + 1)],
                            xn(dc)[:, 512 * tp:512 * (tp + 1)],
                            start=(dc == 0), stop=(dc == NDC - 1)),
                            wait=([tok_w, tok_xn_all, rfree[k]] + list(extra_w)
                                  if dc == 0 else []),
                            inc="t" if dc == NDC - 1 else None)
                    tev = p.on("scalar", lambda e, ps=ps, do=do, tp=tp, dest=dest,
                               bias_col=bias_col, scale=scale: e.activation(
                                   dest[:, 2048 * do + 512 * tp:2048 * do + 512 * (tp + 1)],
                                   ps, AF.Identity, bias=bias_col(do), scale=scale),
                               wait=[lastmm], inc="s")
                    rfree[k] = tev
                    toks.append(tev)
            return toks, lastmm

        qtoks, tok_q_T = proj(wslot[0], tok_wq, qv,
                              lambda do: self.bq8c[:, do:do + 1], 0.125,
                              extra_w=[tok_x_spill, tok_gp_done, tok_bq8])
        ktoks, tok_k_T = proj(wslot[1], tok_wk, kv,
                              lambda do: self.pcols[:, 0 + do:1 + do], 1.0,
                              extra_w=[tok_x_spill, tok_gp_done, self.tok_pcols])
        tok_wv = p.dma("gpsimd", wslot[0][:, :, :],
                       self.Wv_d.rearrange("(c p) f -> p c f", p=128),
                       wait=[tok_q_T])
        vaug = self.AR3[:, 0:16640]
        tok_v = []
        vfree = {0: self.tok_psum_free, 1: self.tok_psum_free}
        for tc in range(NTC):
            ps = [self.pBq[0], self.pBq[1]] if tc % 2 == 0 else [self.pC, self.pD]
            lastmm = None
            for dc in range(NDC):
                for hh in range(2):
                    lastmm = p.on("tensor", lambda e, dc=dc, tc=tc, hh=hh, ps=ps: e.matmul(
                        ps[hh], xn(dc)[:, 128 * tc:128 * (tc + 1)],
                        wslot[0][:, dc, 512 * hh:512 * (hh + 1)],
                        start=(dc == 0), stop=(dc == NDC - 1)),
                        wait=([tok_wv, vfree[tc % 2]] if dc == 0 and hh == 0 else []),
                        inc="t" if (dc == NDC - 1 and hh == 1) else None)
            dst = vaug[:, 1040 * tc:1040 * (tc + 1)].rearrange("p (h x) -> p h x", h=16)
            tvv = None
            for hh in range(2):
                src = ps[hh].rearrange("p (h x) -> p h x", h=8)
                bvv = bvb[:, 512 * hh:512 * (hh + 1)].rearrange("p (h x) -> p h x", h=8)
                tvv = p.on("vector", lambda e, src=src, dst=dst, hh=hh, bvv=bvv: e.tensor_tensor(
                    dst[:, 8 * hh:8 * (hh + 1), 0:64], src, bvv, ALU.add),
                    wait=[lastmm, tok_bvb, self.tok_varena_free if tc == 0 and hh == 0 else None],
                    inc="v")
            tm = p.on("vector", lambda e, dst=dst: e.memset(dst[:, :, 64:65], 1.0),
                      wait=[tvv], inc="v")
            vfree[tc % 2] = tm
            tok_v.append(tm)
        self.tok_qkv = (qtoks[-1], ktoks[-1], tok_v[-1])
        self.tok_qkv_T_last = lastmm
        return qtoks[-1], ktoks[-1], tok_v[-1]

    def attn_phase(self, tok_q, tok_k, tok_v):
        p = self
        kv_off = 16384
        wide = self.AR6[:, 0:7936].bitcast(F32)
        widb = self.AR6[:, 0:3968]
        t_wa = p.dma("sync", wide, self.widea_d, wait=[self.tok_wide_arena_free])
        acol = self.sm[:, 96:97]
        t_ds = p.dma("sync", acol, self.ds_d.broadcast_to([128, 1]))
        t_a = p.on("vector", lambda e: e.tensor_scalar_mul(acol, acol, -0.125),
                   wait=[t_ds], inc="v")
        t_b = p.on("vector", lambda e: e.tensor_single_scalar(
            wide, wide, acol, ALU.mult), wait=[t_wa, t_a], inc="v")
        mbcol = self.sm[:, 100:101]
        t_mb = p.on("vector", lambda e: e.reduce_max(mbcol, wide, axis=AX.X),
                    wait=[t_b], inc="v")
        t_tr = p.on("tensor", lambda e: e.transpose(
            self.pD[0:1, 0:128], mbcol, self.identf[:, :]),
            wait=[t_mb, self.tok_psum_free], inc="t")
        mbrow = self.row_slot[:, 0:128]
        t_cp = p.on("vector", lambda e: e.tensor_copy(mbrow, self.pD[0:1, 0:128]),
                    wait=[t_tr], inc="v")
        mbs = self.sm[0:1, 104:105]
        t_rd = p.on("vector", lambda e: e.reduce_max(mbs, mbrow, axis=AX.X,
                                                     negate=True),
                    wait=[t_cp], inc="v")
        t_bc = p.on("tensor", lambda e: e.matmul(
            self.pD[:, 0:1], self.onesr[:, :], mbs, start=True, stop=True),
            wait=[t_rd], inc="t")
        nmb = self.sm[:, 108:109]
        t_nmb = p.on("vector", lambda e: e.tensor_copy(nmb, self.pD[:, 0:1]),
                     wait=[t_bc], inc="v")
        t_dp = p.on("scalar", lambda e: e.activation(widb, wide, AF.Exp,
                                                     bias=nmb, scale=1.0),
                    wait=[t_nmb], inc="s")
        self.tok_dprime = t_dp

        slots = [self.pA[:, 0:1024], self.pA[:, 1024:2048], self.pB[:, 0:1024]]
        accs = [self.pB[:, 1024:1536], self.pB[:, 1536:2048]]
        ptb = [self.AR5[:, 1024 * k:1024 * (k + 1)] for k in range(3)]
        ebuf = [self.AR5[:, 3072 + 1024 * k:3072 + 1024 * (k + 1)] for k in range(3)]
        linv = self.AR5[:, 6144:7168].bitcast(F32)
        lrow = self.row_slot[:, 0:512]

        def head_ap(base, h, lo, sz):
            po = 64 * (h % 2)
            dc = h // 2
            return self.AR1[po:po + 64, base + 2048 * dc + lo:base + 2048 * dc + lo + sz]

        slot_free = [self.tok_psum_free] * 3
        e_free = [None] * 3
        pt_free = [None] * 3
        acc_free = [self.tok_psum_free, t_nmb]
        tok_a_evac = None
        tok_last_T = None
        LAG = 2

        for pair in range(8):
            hA, hB = 2 * pair, 2 * pair + 1
            for tb in range(4):
                tok_pv = [{}, {}]
                for uc in range(NTC):
                    sl = uc % 3
                    po = slots[sl]
                    lastmm = None
                    for hh, h in enumerate((hA, hB)):
                        lastmm = p.on("tensor", lambda e, h=h, po=po, hh=hh, uc=uc, tb=tb: e.matmul(
                            po[:, 512 * hh:512 * (hh + 1)],
                            head_ap(kv_off, h, 128 * uc, 128),
                            head_ap(0, h, 512 * tb, 512),
                            start=True, stop=True,
                            tile_position=(64 * (h % 2), 0)),
                            wait=([slot_free[sl], tok_q, tok_k, self.tok_dprime]
                                  if hh == 0 else []),
                            inc="t" if hh == 1 else None)
                    te = p.on("scalar", lambda e, po=po, sl=sl: e.activation(
                        ebuf[sl], po, AF.Exp, bias=-12.0, scale=1.0),
                        wait=[lastmm, e_free[sl]], inc="s")
                    slot_free[sl] = te
                    off = 1920 + 512 * tb - 128 * uc
                    tv = None
                    for hh in range(2):
                        tv = p.on("vector", lambda e, sl=sl, off=off, hh=hh: e.tensor_tensor(
                            ptb[sl][:, 512 * hh:512 * (hh + 1)],
                            ebuf[sl][:, 512 * hh:512 * (hh + 1)],
                            widb[:, off:off + 512], ALU.mult),
                            wait=[te, pt_free[sl]] if hh == 0 else [],
                            inc="v" if hh == 1 else None)
                    if uc == 0:
                        if hA > 0:
                            tv = p.on("vector", lambda e, sl=sl, hA=hA: e.memset(
                                ptb[sl][0:hA, 0:512], 0.0), wait=[tv], inc="v")
                        tv = p.on("vector", lambda e, sl=sl, hB=hB: e.memset(
                            ptb[sl][0:hB, 512:1024], 0.0), wait=[tv], inc="v")
                    tok_tv = tv
                    e_free[sl] = tv
                    pvlast = None
                    for hh, h in enumerate((hA, hB)):
                        pvlast = p.on("tensor", lambda e, sl=sl, hh=hh, h=h, uc=uc: e.matmul(
                            accs[hh][0:65, :],
                            self.av_slice(uc, h),
                            ptb[sl][:, 512 * hh:512 * (hh + 1)],
                            start=(uc == 0), stop=(uc == NTC - 1)),
                            wait=[tok_tv, tok_v] +
                                 ([acc_free[hh]] if uc == 0 else []),
                            inc="t" if hh == 1 else None)
                    tok_pv = pvlast
                    pt_free[sl] = pvlast
                tok_last_T = tok_pv
                for hh, h in enumerate((hA, hB)):
                    ps = accs[hh]
                    t1 = p.on("vector", lambda e, ps=ps: e.tensor_single_scalar(
                        lrow, ps[64:65, :], 1e-30, ALU.add),
                        wait=[tok_pv], inc="v")
                    t2a = p.on("scalar", lambda e: e.activation(
                        lrow, lrow, AF.Ln), wait=[t1], inc="s")
                    t2 = p.on("scalar", lambda e: e.activation(
                        lrow, lrow, AF.Exp, bias=0.0, scale=-1.0),
                        wait=[t2a], inc="s")
                    t3 = p.on("tensor", lambda e: e.matmul(
                        self.pA[0:64, 0:512], self.onesr[0:1, 0:64], lrow,
                        start=True, stop=True),
                        wait=[t2, slot_free[0]], inc="t")
                    t4 = p.on("vector", lambda e: e.tensor_copy(
                        linv[0:64, :], self.pA[0:64, 0:512]),
                        wait=[t3], inc="v")
                    slot_free[0] = t4
                    dc = h // 2
                    po_ = 64 * (h % 2)
                    dst = self.AR2[po_:po_ + 64,
                                   2048 * dc + 512 * tb:2048 * dc + 512 * (tb + 1)]
                    t5 = p.on("vector", lambda e, ps=ps, dst=dst: e.tensor_tensor(
                        dst, ps[0:64, :], linv[0:64, :], ALU.mult),
                        wait=[t4, self.tok_a_arena_free
                              if (pair == 0 and tb == 0) else None],
                        inc="v")
                    acc_free[hh] = t5
                    tok_a_evac = t5
        self.tok_attn_A = tok_a_evac
        self.tok_attn_T = tok_last_T
        self.tok_pv_free_final = list(acc_free) + list(acc_free)
        self.tok_pa_free_final = list(slot_free)
        return tok_a_evac

    def av_slice(self, uc, h):
        return self.AR3[:, 1040 * uc + 65 * h:1040 * uc + 65 * (h + 1)]

    def wo_ln1_phase(self):
        p = self
        wslot = self.AR4[:, 0:8192].rearrange("p (c f) -> p c f", c=8)
        tok_wo = p.dma("gpsimd", wslot[:, :, :],
                       self.Wo_d.rearrange("(c p) f -> p c f", p=128),
                       wait=[self.tok_qkv_T_last])
        mu2b = self.AR3[:, 0:4096].bitcast(F32)
        r2b = self.AR3[:, 4096:8192].bitcast(F32)
        row = self.row_slot
        tok_tiles = []
        tpsg = list(self.tok_pa_free_final) + list(self.tok_pv_free_final)
        for i, dstt in enumerate((mu2b, r2b)):
            trr = p.dma("sync", row, self.rowsp_d[:, T * i:T * (i + 1)],
                        wait=[self.tok_attn_A] + (tok_tiles[-4:] if i else []))
            for q in range(4):
                tb = p.on("tensor", lambda e, q=q: e.matmul(
                    self.pAq[q], self.onesr[:, :], row[:, 512 * q:512 * (q + 1)],
                    start=True, stop=True),
                    wait=[trr] + (tpsg if i == 0 and q == 0 else []), inc="t")
                te = p.on("scalar", lambda e, q=q, dstt=dstt: e.copy(
                    dstt[:, 512 * q:512 * (q + 1)], self.pAq[q]),
                    wait=[tb], inc="s")
                tok_tiles.append(te)
        x1r = [self.AR6[:, 0:4096].bitcast(F32), self.AR6[:, 4096:8192].bitcast(F32)]
        tsc = [self.AR5[:, 0:1024].bitcast(F32), self.AR5[:, 1024:2048].bitcast(F32)]
        usc = [self.AR5[:, 4096:5120].bitcast(F32), self.AR5[:, 5120:6144].bitcast(F32)]
        tok_x1 = [None] * NDC
        tok_R = []
        ring = [self.pAq[0], self.pAq[1], self.pAq[2], self.pAq[3]]
        rfree = [tok_tiles[-1]] * 4
        tfree = [None, None]
        ufree = [None, None]
        un = [0]
        last_wo_T = None
        for do in range(NDC):
            tok_x1[do] = p.dma(
                "sync", x1r[do % 2],
                self.x1sp_d[:, 2048 * do:2048 * (do + 1)],
                wait=[self.tok_x1_spilled, self.tok_attn_T,
                      tok_R[-5] if len(tok_R) >= 5 else None])
            for tp in range(NTQ):
                k = un[0] % 4; un[0] += 1
                ps = ring[k]
                lastmm = None
                for dc in range(NDC):
                    lastmm = p.on("tensor", lambda e, dc=dc, do=do, tp=tp, ps=ps: e.matmul(
                        ps, wslot[:, dc, 128 * do:128 * (do + 1)],
                        self.AR2[:, 2048 * dc + 512 * tp:2048 * dc + 512 * (tp + 1)],
                        start=(dc == 0), stop=(dc == NDC - 1)),
                        wait=([tok_wo, self.tok_attn_A, rfree[k]] if dc == 0 else []),
                        inc="t" if dc == NDC - 1 else None)
                last_wo_T = lastmm
                tg = p.on("gpsimd", lambda e, do=do, tp=tp: e.tensor_sub(
                    tsc[tp % 2], x1r[do % 2][:, 512 * tp:512 * (tp + 1)],
                    mu2b[:, 512 * tp:512 * (tp + 1)]),
                    wait=[tok_x1[do], tok_tiles[-1], tfree[tp % 2]], inc="g")
                tu = p.on("vector", lambda e, tp=tp: e.tensor_mul(
                    usc[tp % 2], tsc[tp % 2], r2b[:, 512 * tp:512 * (tp + 1)]),
                    wait=[tg, ufree[tp % 2]], inc="v")
                tfree[tp % 2] = tu
                tR = p.on("vector", lambda e, ps=ps, do=do, tp=tp: e.scalar_tensor_tensor(
                    self.xr[:, 2048 * do + 512 * tp:2048 * do + 512 * (tp + 1)],
                    ps, self.pcols[:, 8 + do:9 + do], usc[tp % 2],
                    ALU.add, ALU.add),
                    wait=[lastmm, tu],
                    inc="v")
                ufree[tp % 2] = tR
                rfree[k] = tR
                tok_R.append(tR)
        self.tok_R = tok_R[-1]
        self.tok_wo_T = last_wo_T
        mub = self.AR5[:, 0:4096].bitcast(F32)
        rb = self.AR5[:, 4096:8192].bitcast(F32)
        xbf_ring = [self.AR6[:, 0:2048], self.AR6[:, 2048:4096]]
        tsc_ring = [self.AR6[:, 0:4096].bitcast(F32),
                    self.AR6[:, 4096:8192].bitcast(F32)]
        xfn = lambda dc: self.xr[:, 2048 * dc:2048 * (dc + 1)]
        xafn = lambda dc: self.AR2[:, 2048 * dc:2048 * (dc + 1)]
        tok_xa, tok_xa_all, tok_gp = self.ln(
            xfn, xafn, self.tok_R, mub, rb, xbf_ring, tsc_ring,
            gamma_col=lambda dc: self.pcols[:, 56 + dc:57 + dc],
            beta_col=lambda dc: self.pcols[:, 64 + dc:65 + dc],
            extra_wait=[self.tok_R],
            psum_guard=[self.tok_R],
            apply_wait=[last_wo_T],
            row_wait=[tok_tiles[-1]])
        self.tok_xa = tok_xa
        self.tok_ln1_gp = tok_gp
        return tok_xa

    def ffn_ln2_phase(self):
        p = self
        w1s = [self.AR4[:, 0:4096].rearrange("p (c f) -> p c f", c=8),
               self.AR4[:, 4096:8192].rearrange("p (c f) -> p c f", c=8)]
        w2s = [self.AR4[:, 8192:12288].rearrange("p (c f) -> p c f", c=32),
               self.AR4[:, 12288:16384].rearrange("p (c f) -> p c f", c=32)]
        h2 = self.AR3[:, 0:16384]
        R2 = self.AR5[:, 0:8192].bitcast(F32)
        xbf = [self.AR6[:, 0:512], self.AR6[:, 512:1024]]
        sqr = [self.AR6[:, 1024:1536], self.AR6[:, 1536:2048]]
        x1r = [self.AR6[:, 2048:3072].bitcast(F32), self.AR6[:, 3072:4096].bitcast(F32)]
        zsc = [self.AR6[:, 4096:5120].bitcast(F32), self.AR6[:, 5120:6144].bitcast(F32)]
        tscr = [self.AR6[:, 6144:7168].bitcast(F32), self.AR6[:, 7168:8192].bitcast(F32)]
        mub2 = self.AR7[:, 0:1024].bitcast(F32)
        rb2 = self.AR7[:, 1024:2048].bitcast(F32)
        rowq = self.AR7[0:1, 2048:3072].bitcast(F32)

        tok_w1 = [None, None]
        tok_w2 = [None, None]
        w1_free = [self.tok_wo_T, self.tok_wo_T]
        w2_free = [self.tok_wo_T, self.tok_wo_T]
        pa_free = {0: self.tok_xa, 1: self.tok_xa}
        ys_free = [self.tok_xa] * 4
        r2_free = None
        tok_x2 = None
        xa = lambda dc: self.AR2[:, 2048 * dc:2048 * (dc + 1)]
        un = [0]
        x1_free = [self.tok_xa, self.tok_xa]
        tok_ln2_prev = None
        for tq in range(NTQ):
            tok_h2 = None
            for fg in range(8):
                j = fg % 2
                tok_w1[j] = p.dma("sync", w1s[j][:, :, :],
                                  self.wc_W1a_d[:, 512 * fg:512 * (fg + 1)].rearrange(
                                      "(c p) f -> p c f", p=128),
                                  wait=[w1_free[j], self.tok_wc_ffn])
                for fcp in range(2):
                    u = un[0] % 2; un[0] += 1
                    pp = [self.pAq[2 * u], self.pAq[2 * u + 1]]
                    lastmm = None
                    for dc in range(NDC):
                        for f2 in range(2):
                            fc = fcp * 2 + f2
                            lastmm = p.on("tensor", lambda e, j=j, dc=dc, fc=fc, tq=tq, ps=pp[f2]: e.matmul(
                                ps, w1s[j][:, dc, 128 * fc:128 * (fc + 1)],
                                xa(dc)[:, 512 * tq:512 * (tq + 1)],
                                start=(dc == 0), stop=(dc == NDC - 1)),
                                wait=([tok_w1[j], self.tok_xa, pa_free[u]]
                                      if dc == 0 and f2 == 0 else []),
                                inc="t" if (dc == NDC - 1 and f2 == 1) else None)
                    w1_free[j] = lastmm
                    last_ev = None
                    for f2 in range(2):
                        fc = fcp * 2 + f2
                        gfc = fg * 4 + fc
                        last_ev = p.on("scalar", lambda e, gfc=gfc, ps=pp[f2]: e.activation(
                            h2[:, 512 * gfc:512 * (gfc + 1)], ps, AF.Relu,
                            bias=self.pcols[:, 16 + gfc:17 + gfc], scale=1.0),
                            wait=[lastmm, self.tok_pcols,
                                  self.tok_h2_dead if tq == 0 and gfc == 0 else None],
                            inc="s")
                        tok_h2 = last_ev
                    pa_free[u] = last_ev
            tok_R2 = []
            for do in range(NDC):
                j = do % 2
                tok_w2[j] = p.dma("sync", w2s[j][:, :, :],
                                  self.wc_W2a_d[:, 128 * do:128 * (do + 1)].rearrange(
                                      "(c p) f -> p c f", p=128),
                                  wait=[w2_free[j], self.tok_wc_ffn])
                k = (do + 4 * tq) % 4
                ps = self.ys_ring[k]
                lastmm = None
                for fc in range(32):
                    lastmm = p.on("tensor", lambda e, fc=fc, do=do, j=j, ps=ps: e.matmul(
                        ps, w2s[j][:, fc, :],
                        h2[:, 512 * fc:512 * (fc + 1)],
                        start=(fc == 0), stop=(fc == 31)),
                        wait=([tok_w2[j], tok_h2, ys_free[k]] if fc == 0 else []),
                        inc="t" if fc == 31 else None)
                w2_free[j] = lastmm
                tR2 = p.on("vector", lambda e, ps=ps, do=do, tq=tq: e.scalar_tensor_tensor(
                    R2[:, 512 * do:512 * (do + 1)], ps,
                    self.pcols[:, 48 + do:49 + do],
                    xa(do)[:, 512 * tq:512 * (tq + 1)], ALU.add, ALU.add),
                    wait=[lastmm, r2_free, self.tok_pcols], inc="v")
                ys_free[k] = tR2
                tok_R2.append(tR2)
            tok_cast = [None] * NDC
            tok_mm = [None] * NDC
            for dc in range(NDC):
                xb = xbf[dc % 2]
                tok_cast[dc] = p.on("scalar", lambda e, xb=xb, dc=dc: e.copy(
                    xb, R2[:, 512 * dc:512 * (dc + 1)]),
                    wait=[tok_R2[dc], tok_mm[dc - 2] if dc >= 2 else None], inc="s")
                sq = sqr[dc % 2]
                tsq = p.on("vector", lambda e, xb=xb, sq=sq: e.tensor_mul(sq, xb, xb),
                           wait=[tok_cast[dc]], inc="v")
                p.on("tensor", lambda e, xb=xb, dc=dc: e.matmul(
                    self.pAq[0][0:1, :], self.onescb[:, :], xb,
                    start=(dc == 0), stop=(dc == NDC - 1)),
                    wait=[tok_cast[dc]] + ([pa_free[0], pa_free[1]] if dc == 0 else []))
                tok_mm[dc] = p.on("tensor", lambda e, sq=sq, dc=dc: e.matmul(
                    self.pAq[1][0:1, :], self.onescb[:, :], sq,
                    start=(dc == 0), stop=(dc == NDC - 1)),
                    wait=[tsq], inc="t")
            tok_sums = tok_mm[NDC - 1]
            rowq2 = self.AR7[0:1, 3072:4096].bitcast(F32)
            t1 = p.on("vector", lambda e: e.tensor_copy(rowq, self.pAq[0][0:1, :]),
                      wait=[tok_sums], inc="v")
            t1b = p.on("vector", lambda e, rowq2=rowq2: e.tensor_copy(
                rowq2, self.pAq[1][0:1, :]), wait=[t1], inc="v")
            t2 = p.on("vector", lambda e: e.tensor_scalar_mul(rowq, rowq, 1.0 / D),
                      wait=[t1b], inc="v")
            tb1 = p.on("tensor", lambda e: e.matmul(
                self.pAq[0], self.onesr[:, :], rowq, start=True, stop=True),
                wait=[t2], inc="t")
            te1 = p.on("scalar", lambda e: e.copy(mub2[:, 0:512], self.pAq[0]),
                       wait=[tb1], inc="s")
            t3 = p.on("vector", lambda e: e.tensor_mul(rowq, rowq, rowq),
                      wait=[tb1], inc="v")
            t5 = p.on("vector", lambda e, rowq2=rowq2: e.scalar_tensor_tensor(
                rowq2, rowq2, 1.0 / D, rowq, ALU.mult, ALU.subtract),
                wait=[t3], inc="v")
            t6 = p.on("scalar", lambda e, rowq2=rowq2: e.activation(
                rowq2, rowq2, AF.Ln, bias=LN_EPS, scale=1.0),
                wait=[t5], inc="s")
            t7 = p.on("scalar", lambda e, rowq2=rowq2: e.activation(
                rowq2, rowq2, AF.Exp, bias=0.0, scale=-0.5),
                wait=[t6], inc="s")
            tb2 = p.on("tensor", lambda e, rowq2=rowq2: e.matmul(
                self.pAq[1], self.onesr[:, :], rowq2, start=True, stop=True),
                wait=[t7], inc="t")
            te2 = p.on("scalar", lambda e: e.copy(rb2[:, 0:512], self.pAq[1]),
                       wait=[tb2], inc="s")
            pa_free = {0: te2, 1: pa_free[1]}
            un[0] += (un[0] + 1) % 2
            tok_ln2_prev = None
            for dc in range(NDC):
                tok_x1q = p.dma("sync", x1r[dc % 2],
                                self.x1sp_d[:, 2048 * dc + 512 * tq:
                                            2048 * dc + 512 * (tq + 1)],
                                wait=[x1_free[dc % 2]])
                ts_ = tscr[dc % 2]
                tg = p.on("gpsimd", lambda e, ts_=ts_, dc=dc: e.tensor_sub(
                    ts_, R2[:, 512 * dc:512 * (dc + 1)], mub2[:, 0:512]),
                    wait=[te1, te2, tok_ln2_prev], inc="g")
                tu = p.on("vector", lambda e, ts_=ts_: e.tensor_mul(
                    ts_, ts_, rb2[:, 0:512]), wait=[tg], inc="v")
                z = zsc[dc % 2]
                tz = p.on("scalar", lambda e, ts_=ts_, z=z, dc=dc: e.activation(
                    z, ts_, AF.Identity, bias=self.pcols[:, 80 + dc:81 + dc],
                    scale=self.pcols[:, 72 + dc:73 + dc]),
                    wait=[tu], inc="s")
                tok_x2 = p.on("vector", lambda e, z=z, dc=dc, tq=tq: e.tensor_add(
                    self.xr[:, 2048 * dc + 512 * tq:2048 * dc + 512 * (tq + 1)],
                    z, x1r[dc % 2]),
                    wait=[tz, tok_x1q, self.tok_R], inc="v")
                x1_free[dc % 2] = tok_x2
                tok_ln2_prev = tok_x2
            r2_free = tok_x2
        self.tok_x2 = tok_x2
        self.tok_ffn_T_list = [w1_free[0], w1_free[1]]
        return tok_x2

    def phase_out(self, tok_x_final, tok_psum_free=(), tok_per_tq=None):
        p = self
        oring = [self.AR6[:, 0:2048].bitcast(F32),
                 self.AR6[:, 2048:4096].bitcast(F32)]
        tok_out_dma = [None] * NTC
        prev_ev = {}
        for tc in range(NTC):
            slot = oring[tc % 2]
            last_ev = None
            tok_x = tok_per_tq[tc // 4] if tok_per_tq else tok_x_final
            for dc in range(NDC):
                widx = tc * NDC + dc
                ps = self.pD[:, 0:128] if widx % 2 == 0 else self.pC[:, 0:128]
                tt = p.on("tensor", lambda e, ps=ps, dc=dc, tc=tc: e.transpose(
                    ps, self.xr[:, 2048 * dc + 128 * tc:2048 * dc + 128 * (tc + 1)],
                    self.identf[:, :]),
                    wait=[tok_x, prev_ev.get(widx - 2),
                          tok_out_dma[tc - 2] if tc >= 2 else None] +
                         (list(tok_psum_free) if widx == 0 else []),
                    inc="t")
                last_ev = p.on("vector", lambda e, ps=ps, dc=dc, slot=slot: e.tensor_copy(
                    slot[:, 128 * dc:128 * (dc + 1)], ps),
                    wait=[tt], inc="v")
                prev_ev[widx] = last_ev
            tok_out_dma[tc] = p.dma(
                "sync", self.out_d[128 * tc:128 * (tc + 1), :], slot,
                wait=[last_ev])

    def build(self):
        nc = self.nc
        self.declare()
        import contextlib
        es = contextlib.ExitStack()
        sb = lambda name, shape, dtype: es.enter_context(nc.sbuf_tensor(name, shape, dtype))
        self.AR1 = sb("AR1", [128, 2 * NDC * T], BF16)
        self.AR2 = sb("AR2", [128, NDC * T], BF16)
        self.AR3 = sb("AR3", [128, 18432], BF16)
        self.AR4 = sb("AR4", [128, NDC * T], BF16)
        self.AR5 = sb("AR5", [128, 8192], BF16)
        self.AR6 = sb("AR6", [128, 8192], BF16)
        self.AR7 = sb("AR7", [128, 4608], BF16)
        self.identf = sb("identfs", [128, 128], F32)
        self.onesr = sb("onesrs", [1, 128], F32)
        self.onesrb = sb("onesrbs", [1, 128], BF16)
        self.onescb = sb("onescbs", [128, 1], BF16)
        self.wrsb = sb("wrsbs", [128, 64], BF16)
        self.pcols = sb("pcolss", [128, 96], F32)[:, :]
        self.pA = es.enter_context(nc.psum_tensor("pA", [128, 2048], F32))
        self.pB = es.enter_context(nc.psum_tensor("pB", [128, 2048], F32))
        self.pC = self.pB[:, 1024:1536]
        self.pD = self.pB[:, 1536:2048]
        sems = {}
        for s in ("t", "v", "s", "g", "ds", "dg", "dsc"):
            sems[s] = es.enter_context(nc.semaphore("sem_" + s))
        block = es.enter_context(nc.Block())

        self.views()

        self.tok_consts = None
        for i, cval in enumerate([LN_EPS, -0.5, 0.125, -12.0, -1.0]):
            col = self.sm[:, 66 + i:67 + i]
            self.tok_consts = self.on(
                "vector", lambda e, col=col, cval=cval: e.memset(col, cval),
                inc="v")
            nc.const_aps.register(F32, cval, col)

        self.tok_ident = self.dma("sync", self.identf[:, :], self.identf_d)
        self.tok_onesr = self.dma("sync", self.onesr[:, :], self.onesr_d)
        self.tok_onesrb = self.dma("sync", self.onesrb[:, :], self.onesrb_d)
        self.tok_onescb = self.dma("sync", self.onescb[:, :], self.onescb_d)
        self.tok_wr = self.dma("gpsimd",
                               self.wrsb[:, :].rearrange("p (c f) -> p c f", c=8),
                               self.Wr_d.rearrange("(c p) f -> p c f", p=128))
        self.tok_bq = self.dma(
            "sync", self.bq8c, self.bq_d.rearrange("o (c p) -> p (o c)", p=128),
            slow=True)
        pc = self.pcols
        tpc = None
        for dst_lo, src in ((0, self.bk_d), (8, self.bo_d), (16, self.b1a_d),
                            (48, self.b2a_d), (56, self.g1_d), (64, self.be1_d),
                            (72, self.g2_d), (80, self.be2_d)):
            w_ = src.shape[1] // 128
            tpc = self.dma("sync", pc[:, dst_lo:dst_lo + w_],
                           src.rearrange("o (c p) -> p (o c)", p=128), slow=True)
        self.tok_pcols_l = tpc

        self.phase_in()

        mub = self.AR5[:, 0:4096].bitcast(F32)
        rb = self.AR5[:, 4096:8192].bitcast(F32)
        xbf_ring = [self.AR6[:, 0:2048], self.AR6[:, 2048:4096]]
        self.sq_ring = [self.AR6[:, 4096:6144], self.AR6[:, 6144:8192]]
        tsc_ring = [self.AR6[:, 0:4096].bitcast(F32),
                    self.AR6[:, 4096:8192].bitcast(F32)]
        xfn = lambda dc: self.xr[:, 2048 * dc:2048 * (dc + 1)]
        xnfn = lambda dc: self.AR2[:, 2048 * dc:2048 * (dc + 1)]
        tok_xn_last, tok_xn_all_l, tok_ln_gp = self.ln(
            xfn, xnfn, self.tok_x_ready, mub, rb, xbf_ring, tsc_ring,
            extra_wait=[self.tok_onescb, self.tok_onesr, self.tok_in_trans_done])
        self.dbg(self.AR2[:, :], wait=[tok_xn_last])
        self.dbg(mub[:, :], wait=[tok_xn_last])
        self.dbg(rb[:, :], wait=[tok_xn_last])
        tok_moe_x, tok_moe_T, tok_moe_tq = self.moe(
            xnfn, tok_xn_last, tok_xn_last, tok_arena_free=(), tag=0,
            tok_xres=self.tok_x_ready)

        tok_x_final = tok_moe_x
        tok_out_psum_guard = [tok_moe_T]
        if STAGE >= 2:
            tok_xn2, tok_xn2_all, tok_lnb_gp = self.ln(
                xfn, xnfn, tok_moe_x, mub, rb, xbf_ring, tsc_ring,
                spill=True,
                extra_wait=[tok_moe_x],
                psum_guard=[tok_moe_x],
                row_wait=[])
            self.tok_x1_spilled = self.dma("sync", self.x1sp_d, self.xr,
                                           wait=[tok_moe_x])
            self.tok_w_arena_free = tok_moe_T
            self.tok_psum_free = tok_moe_x
            self.tok_varena_free = tok_moe_T
            self.tok_bvb_free = tok_lnb_gp
            self.tok_wide_arena_free = tok_xn2
            self.tok_pcols = self.tok_pcols_l
            tq_, tk_, tv_ = self.qkv_phase(
                xnfn, tok_xn2, self.tok_x1_spilled, tok_lnb_gp)
            self.tok_wc_ffn = None
            for dst, src in ((self.wc_W1a_d, self.W1a_d),
                             (self.wc_W2a_d, self.W2a_d)):
                self.tok_wc_ffn = self.dma("gpsimd", dst, src)
            self.tok_wc_moe = None
            for dst, src in ((self.wc_W1e_d, self.W1e_d),
                             (self.wc_W2e_d, self.W2e_d),
                             (self.wc_Ws1_d, self.Ws1_d),
                             (self.wc_Ws2_d, self.Ws2_d)):
                self.tok_wc_moe = self.dma("gpsimd", dst, src)
            self.tok_a_arena_free = self.tok_qkv_T_last
            self.attn_phase(tq_, tk_, tv_)
            self.wo_ln1_phase()
            self.tok_h2_dead = self.tok_R
            self.ffn_ln2_phase()
            tok_x_final = self.tok_x2
            tok_out_psum_guard = [self.tok_x2]

        if STAGE >= 3:
            tok_xn3, _, tok_lnc_gp = self.ln(
                xfn, xnfn, self.tok_x2, mub, rb, xbf_ring, tsc_ring,
                extra_wait=[self.tok_x2],
                psum_guard=[self.tok_x2],
                apply_wait=self.tok_ffn_T_list,
                row_wait=[self.tok_x2])
            tok_moe2_x, tok_moe2_T, tok_moe2_tq = self.moe(
                xnfn, tok_xn3, tok_xn3,
                tok_arena_free=[self.tok_x2], tag=1,
                tok_xres=self.tok_x2, cached=True, tok_cache=self.tok_wc_moe)
            tok_x_final = tok_moe2_x
            tok_out_psum_guard = [tok_moe2_T, tok_moe2_x]
            self.phase_out(tok_x_final, tok_psum_free=tok_out_psum_guard,
                           tok_per_tq=None)
        else:
            self.phase_out(tok_x_final, tok_psum_free=tok_out_psum_guard)

        self.emit(block, sems)
        es.close()
        return nc


def build_nc():
    return Kern().build()


_WIDE_CONST = None


def _consts():
    global _WIDE_CONST
    if _WIDE_CONST is None:
        pcol = np.arange(128, dtype=np.float32)[:, None]
        j = np.arange(WIDE, dtype=np.float32)[None, :]
        _WIDE_CONST = np.abs(pcol - (j - 1920.0)).astype(np.float32)
    return {
        "identf": np.eye(128, dtype=np.float32),
        "onesr": np.ones((1, 128), dtype=np.float32),
        "onesrb": np.ones((1, 128), dtype=ml_dtypes.bfloat16),
        "onescb": np.ones((128, 1), dtype=ml_dtypes.bfloat16),
        "widea": _WIDE_CONST,
    }


def kernel(**inputs):
    n = 8
    toks = np.ascontiguousarray(inputs["L0_attn_tokens"])
    consts = _consts()

    def r(name):
        return np.ascontiguousarray(inputs[name])

    shared = {
        "Wq": r("L0_attn_Wq"), "bq": r("L0_attn_bq").reshape(1, D),
        "Wk": r("L0_attn_Wk"), "bk": r("L0_attn_bk").reshape(1, D),
        "Wv": r("L0_attn_Wv"), "bv": r("L0_attn_bv").reshape(1, D),
        "Wo": r("L0_attn_Wo"), "bo": r("L0_attn_bo").reshape(1, D),
        "dsc0": r("L0_attn_dist_scale").reshape(1, 1),
        "W1a": r("L0_attn_W1"), "b1a": r("L0_attn_b1").reshape(1, FF),
        "W2a": r("L0_attn_W2"), "b2a": r("L0_attn_b2").reshape(1, D),
        "g1": r("L0_attn_ln1_g").reshape(1, D), "be1": r("L0_attn_ln1_b").reshape(1, D),
        "g2": r("L0_attn_ln2_g").reshape(1, D), "be2": r("L0_attn_ln2_b").reshape(1, D),
        "Ws1": r("L0_ffn_Ws1"), "Ws2": r("L0_ffn_Ws2"),
        "W1e": r("L0_ffn_W1"), "W2e": r("L0_ffn_W2"),
        "Wr": r("L0_ffn_Wr"),
        **consts,
    }
    nc = build_nc()
    in_maps = []
    for c in range(n):
        m = dict(shared)
        m["x0"] = np.ascontiguousarray(toks[c])
        in_maps.append(m)
    res = run_bass_kernel_spmd(nc, in_maps, core_ids=list(range(n)))
    out = np.stack([res.results[c]["out"] for c in range(n)], axis=0)
    if DEBUG:
        kernel._dbg = res.results
    return out


# version 08
# speedup vs baseline: 1.3537x; 1.3537x over previous
import numpy as np
import ml_dtypes
import concourse.bass as bass
import concourse.mybir as mybir
from concourse.bass_utils import run_bass_kernel_spmd

F32 = mybir.dt.float32
BF16 = mybir.dt.bfloat16
AF = mybir.ActivationFunctionType
ALU = mybir.AluOpType
AX = mybir.AxisListType

T = 2048
D = 1024
NDC = 8
H = 16
DH = 64
NE = 8
FE = 512
FF = 4096
NTC = 16
NTQ = 4
WIDE = 3968
LN_EPS = 1e-5
JUNK = 1

STAGE = 3
DEBUG = False

ENGINES = ("sync", "tensor", "vector", "scalar", "gpsimd")
DMA_SEM = {"sync": "ds", "gpsimd": "dg", "scalar": "dsc"}


class Kern:
    def __init__(self):
        self.nc = bass.Bass("TRN2", target_bir_lowering=False)
        self.steps = []
        self.ctr = {}
        self.dbg_names = []

    def dbg(self, ap, wait=()):
        if not DEBUG:
            return None
        name = "dbg%d" % len(self.dbg_names)
        d = self.nc.dram_tensor(name, list(ap.shape), ap.dtype,
                                kind="ExternalOutput").ap()
        self.dbg_names.append(name)
        return self.dma("sync", d, ap, wait=wait)

    def on(self, eng, fn, wait=(), inc=None, inc_amt=1):
        waits = tuple(w for w in wait if w is not None)
        tok = None
        if inc is not None:
            self.ctr[inc] = self.ctr.get(inc, 0) + inc_amt
            tok = (inc, self.ctr[inc])
        self.steps.append((eng, fn, waits, (inc, inc_amt) if inc else None))
        return tok

    def dma(self, eng, out, in_, wait=(), slow=False):
        return self.on(eng, lambda e, o=out, i=in_: e.dma_start(
            o, i, allow_slow_non_contiguous=slow),
            wait=wait, inc=DMA_SEM[eng], inc_amt=16)

    def emit(self, block, sems):
        for s in ("ds", "dg", "dsc"):
            if self.ctr.get(s, 0) > 0:
                v = self.ctr[s]
                self.steps.append(
                    ("sync", lambda e: e.nop(), ((s, v),), None))
        seen = {e: {} for e in ENGINES}

        def mk(eng_name):
            def body(eng):
                for (e, fn, waits, inc) in self.steps:
                    if e != eng_name:
                        continue
                    for (s, v) in waits:
                        if seen[eng_name].get(s, 0) < v:
                            eng.wait_ge(sems[s], v)
                            seen[eng_name][s] = v
                    ins = fn(eng)
                    if inc is not None:
                        ins.then_inc(sems[inc[0]], inc[1])
            return body

        for name in ENGINES:
            getattr(block, name)(mk(name))

    def declare(self):
        nc = self.nc

        def din(name, shape, dtype=F32):
            return nc.dram_tensor(name, shape, dtype, kind="ExternalInput").ap()

        self.x0_d = din("x0", [T, D])
        self.Wq_d = din("Wq", [D, D]); self.bq_d = din("bq", [1, D])
        self.Wk_d = din("Wk", [D, D]); self.bk_d = din("bk", [1, D])
        self.Wv_d = din("Wv", [D, D]); self.bv_d = din("bv", [1, D])
        self.Wo_d = din("Wo", [D, D]); self.bo_d = din("bo", [1, D])
        self.ds_d = din("dsc0", [1, 1])
        self.W1a_d = din("W1a", [D, FF]); self.b1a_d = din("b1a", [1, FF])
        self.W2a_d = din("W2a", [FF, D]); self.b2a_d = din("b2a", [1, D])
        self.g1_d = din("g1", [1, D]); self.be1_d = din("be1", [1, D])
        self.g2_d = din("g2", [1, D]); self.be2_d = din("be2", [1, D])
        self.Ws1_d = din("Ws1", [D, FE]); self.Ws2_d = din("Ws2", [FE, D])
        self.W1e_d = din("W1e", [NE, D, FE]); self.W2e_d = din("W2e", [NE, FE, D])
        self.Wr_d = din("Wr", [D, NE])
        self.identf_d = din("identf", [128, 128])
        self.onesr_d = din("onesr", [1, 128])
        self.onesrb_d = din("onesrb", [1, 128], BF16)
        self.onescb_d = din("onescb", [128, 1], BF16)
        self.widea_d = din("widea", [128, WIDE])

        self.wc_W1a_d = nc.dram_tensor("wcW1a", [D, FF], BF16, kind="Internal").ap()
        self.wc_W2a_d = nc.dram_tensor("wcW2a", [FF, D], BF16, kind="Internal").ap()
        self.wc_W1e_d = nc.dram_tensor("wcW1e", [NE, D, FE], BF16, kind="Internal").ap()
        self.wc_W2e_d = nc.dram_tensor("wcW2e", [NE, FE, D], BF16, kind="Internal").ap()
        self.wc_Ws1_d = nc.dram_tensor("wcWs1", [D, FE], BF16, kind="Internal").ap()
        self.wc_Ws2_d = nc.dram_tensor("wcWs2", [FE, D], BF16, kind="Internal").ap()
        self.out_d = nc.dram_tensor("out", [T, D], F32, kind="ExternalOutput").ap()
        self.x1sp_d = nc.dram_tensor("x1sp", [128, NDC * T], F32, kind="Internal").ap()
        self.rowsp_d = nc.dram_tensor("rowsp", [1, 2 * T], F32, kind="Internal").ap()
        self.probsp_d = nc.dram_tensor("probsp", [NE, T], BF16, kind="Internal").ap()

    def views(self):
        self.xr = self.AR1[:, :].bitcast(F32)
        self.pAq = [self.pA[:, 512 * i:512 * (i + 1)] for i in range(4)]
        self.pBq = [self.pB[:, 512 * i:512 * (i + 1)] for i in range(2)]
        self.ys_ring = [self.pBq[0], self.pBq[1], self.pC, self.pD]

        sm = self.AR7[:, 4096:4608].bitcast(F32)
        self.sm = sm
        self.lt = sm[:, 0:8]
        self.v1c = sm[:, 8:9]; self.v2c = sm[:, 9:10]; self.dcol = sm[:, 10:11]
        self.edc = sm[:, 11:12]; self.denc = sm[:, 12:13]
        self.rw1c = sm[:, 13:14]; self.rw2c = sm[:, 14:15]
        self.p1fc = sm[:, 15:16]; self.p2fc = sm[:, 16:17]
        self.eq1m = sm[:, 17:25]; self.ltmm = sm[:, 25:33]
        self.eq2m = sm[:, 33:41]; self.t2m = sm[:, 41:49]
        self.probm = sm[:, 49:57]
        self.bq8c = sm[:, 57:65]
        self.row_slot = self.AR7[0:1, 0:4096].bitcast(F32)
        self.prob_fm = self.AR7[0:8, 0:2048]

    def phase_in(self):
        p = self
        in_ring = [self.AR6[:, 0:2048].bitcast(F32),
                   self.AR6[:, 2048:4096].bitcast(F32)]
        tok_dma = [None] * NTC
        tok_done = [None] * NTC
        tr_evac = {}
        last_tr = None
        for tc in range(NTC):
            slot = in_ring[tc % 2]
            w = [tok_done[tc - 2]] if tc >= 2 else []
            tok_dma[tc] = p.dma("sync", slot,
                                self.x0_d[128 * tc:128 * (tc + 1), :], wait=w)
            last_v = None
            for dc in range(NDC):
                widx = tc * NDC + dc
                ps = self.pD[:, 0:128] if widx % 2 == 0 else self.pC[:, 0:128]
                last_tr = p.on(
                    "tensor",
                    lambda e, ps=ps, slot=slot, dc=dc: e.transpose(
                        ps, slot[:, 128 * dc:128 * (dc + 1)], self.identf[:, :]),
                    wait=[tok_dma[tc], self.tok_ident,
                          tr_evac.get(widx - 2)],
                    inc="t")
                last_v = p.on(
                    "vector",
                    lambda e, ps=ps, dc=dc, tc=tc: e.tensor_copy(
                        self.xr[:, 2048 * dc + 128 * tc:2048 * dc + 128 * (tc + 1)],
                        ps),
                    wait=[last_tr], inc="v")
                tr_evac[widx] = last_v
            tok_done[tc] = last_v
        self.tok_x_ready = tok_done[-1]
        self.tok_in_trans_done = last_tr

    def ln(self, X, OUT, tok_x, mub, rb, xbf_ring, tsc_ring,
           gamma_col=None, beta_col=None, spill=False, extra_wait=(),
           psum_guard=(), apply_wait=(), row_wait=()):
        p = self
        tok_cast = [None] * NDC
        tok_sq = [None] * NDC
        tok_mm = [None] * NDC
        for dc in range(NDC):
            xb = xbf_ring[dc % 2]
            w = [tok_x] + list(extra_wait) + ([tok_mm[dc - 2]] if dc >= 2 else [])
            tok_cast[dc] = p.on("scalar",
                                lambda e, xb=xb, dc=dc: e.copy(xb, X(dc)),
                                wait=w, inc="s")
            sq = self.sq_ring[dc % 2]
            tok_sq[dc] = p.on("vector",
                              lambda e, xb=xb, sq=sq: e.tensor_mul(sq, xb, xb),
                              wait=[tok_cast[dc]], inc="v")
            last = None
            for q in range(4):
                p.on("tensor",
                     lambda e, xb=xb, q=q, dc=dc: e.matmul(
                         self.pAq[q][0:1, :], self.onescb[:, :],
                         xb[:, 512 * q:512 * (q + 1)],
                         start=(dc == 0), stop=(dc == NDC - 1)),
                     wait=[tok_cast[dc]] + (list(psum_guard) if dc == 0 else []),
                     inc=None)
            s2ps = [self.pBq[0], self.pBq[1], self.pC, self.pD]
            for q in range(4):
                last = p.on("tensor",
                            lambda e, sq=sq, q=q, dc=dc, s2=s2ps[q]: e.matmul(
                                s2[0:1, :], self.onescb[:, :],
                                sq[:, 512 * q:512 * (q + 1)],
                                start=(dc == 0), stop=(dc == NDC - 1)),
                            wait=[tok_sq[dc]],
                            inc="t" if q == 3 else None)
            tok_mm[dc] = last
        tok_sums = tok_mm[NDC - 1]
        row = self.row_slot
        t1 = p.on("vector", lambda e: e.tensor_copy(
            row, self.pA[0:1, 0:2048]), wait=[tok_sums] + list(row_wait), inc="v")
        t2 = p.on("vector", lambda e: e.tensor_scalar_mul(row, row, 1.0 / D),
                  wait=[t1], inc="v")
        tok_sp_mu = None
        if spill:
            tok_sp_mu = p.dma("sync", self.rowsp_d[:, 0:T], row, wait=[t2])
        tok_bc = []
        for q in range(4):
            tb = p.on("tensor", lambda e, q=q: e.matmul(
                self.pAq[q], self.onesr[:, :], row[:, 512 * q:512 * (q + 1)],
                start=True, stop=True), wait=[t2], inc="t")
            te = p.on("scalar", lambda e, q=q, mub=mub: e.copy(
                mub[:, 512 * q:512 * (q + 1)], self.pAq[q]),
                wait=[tb], inc="s")
            tok_bc.append(te)
        t3 = p.on("vector", lambda e: e.tensor_mul(
            self.pA[0:1, 0:2048], row, row), wait=[t2] + tok_bc, inc="v")
        t4 = p.on("vector", lambda e: e.tensor_copy(row[:, 0:512], self.pBq[0][0:1, :]),
                  wait=[t3, tok_sp_mu], inc="v")
        t4b = p.on("vector", lambda e: e.tensor_copy(row[:, 512:1024], self.pBq[1][0:1, :]),
                   wait=[t4], inc="v")
        t4c = p.on("vector", lambda e: e.tensor_copy(row[:, 1024:1536], self.pC[0:1, :]),
                   wait=[t4b], inc="v")
        t4d = p.on("vector", lambda e: e.tensor_copy(row[:, 1536:2048], self.pD[0:1, :]),
                   wait=[t4c], inc="v")
        t5 = p.on("vector", lambda e: e.scalar_tensor_tensor(
            row, row, 1.0 / D, self.pA[0:1, 0:2048], ALU.mult, ALU.subtract),
            wait=[t4d], inc="v")
        t6 = p.on("scalar", lambda e: e.activation(row, row, AF.Ln, bias=LN_EPS, scale=1.0),
                  wait=[t5], inc="s")
        t7 = p.on("scalar", lambda e: e.activation(row, row, AF.Exp, bias=0.0, scale=-0.5),
                  wait=[t6], inc="s")
        tok_sp_r = None
        if spill:
            tok_sp_r = p.dma("sync", self.rowsp_d[:, T:2 * T], row, wait=[t7])
        tok_bc2 = []
        for q in range(4):
            tb = p.on("tensor", lambda e, q=q: e.matmul(
                self.pAq[q], self.onesr[:, :], row[:, 512 * q:512 * (q + 1)],
                start=True, stop=True), wait=[t7, t5], inc="t")
            te = p.on("scalar", lambda e, q=q, rb=rb: e.copy(
                rb[:, 512 * q:512 * (q + 1)], self.pAq[q]),
                wait=[tb], inc="s")
            tok_bc2.append(te)
        tok_tiles = tok_bc + tok_bc2 + [tok_sp_mu, tok_sp_r]
        tok_out = []
        last = None
        tg = None
        for dc in range(NDC):
            ts = tsc_ring[dc % 2]
            tg = p.on("gpsimd", lambda e, ts=ts, dc=dc, mub=mub: e.tensor_sub(
                ts, X(dc), mub),
                wait=tok_tiles + list(apply_wait) +
                     [tok_out[-2] if len(tok_out) >= 2 else None],
                inc="g")
            if gamma_col is None:
                tv = p.on("vector", lambda e, ts=ts, dc=dc, rb=rb: e.tensor_mul(
                    OUT(dc), ts, rb), wait=[tg], inc="v")
            else:
                u = ts
                tv0 = p.on("vector", lambda e, ts=ts, rb=rb: e.tensor_mul(
                    ts, ts, rb), wait=[tg], inc="v")
                tv = p.on("scalar", lambda e, ts=ts, dc=dc: e.activation(
                    OUT(dc), ts, AF.Identity, bias=beta_col(dc), scale=gamma_col(dc)),
                    wait=[tv0], inc="s")
            tok_out.append(tv)
            last = tv
        return last, tok_out, tg

    def moe(self, xn, tok_xn, tok_xn_all, tok_arena_free=(), tag=0,
            tok_xres=None, cached=False, tok_cache=None):
        p = self
        nc = self.nc
        w1v = [self.AR4[:, 0:4096].rearrange("p (c f) -> p c f", c=8),
               self.AR4[:, 4096:8192].rearrange("p (c f) -> p c f", c=8)]
        w2v = [self.AR4[:, 8192:12288].rearrange("p (c f) -> p c f", c=4),
               self.AR4[:, 12288:16384].rearrange("p (c f) -> p c f", c=4)]
        pb_slot = [self.AR6[:, 0:2048], self.AR6[:, 2048:4096]]
        rp_scr = self.AR6[:, 4096 + 512 * 0:4096 + 512 * 1]
        rp_scr2 = self.AR6[:, 4608:5120]
        rps = [rp_scr, rp_scr2]

        h2 = self.AR3[:, 0:16384]

        def h2s(bi, fc, tq):
            off = bi * 8192 + fc * 2048 + tq * 512
            return h2[:, off:off + 512]

        groups = [("s", 0), (1, 2), (3, 4), (5, 6), (7,)]

        weng = "sync" if cached else "gpsimd"

        def w1_dram(br):
            if cached:
                src = self.wc_Ws1_d if br == "s" else self.wc_W1e_d[br]
            else:
                src = self.Ws1_d if br == "s" else self.W1e_d[br]
            return src.rearrange("(c p) f -> p c f", p=128)

        def w2_dram(br):
            if cached:
                src = self.wc_Ws2_d if br == "s" else self.wc_W2e_d[br]
            else:
                src = self.Ws2_d if br == "s" else self.W2e_d[br]
            return src.rearrange("(c p) f -> p c f", p=128)

        tok_lg_evac = None
        tok_tr_evac = None
        tok_probm_free = None
        tok_probs = []
        for tc in range(NTC):
            last = None
            for dc in range(NDC):
                last = p.on("tensor", lambda e, dc=dc, tc=tc: e.matmul(
                    self.pD[:, 0:8],
                    xn(dc)[:, 128 * tc:128 * (tc + 1)],
                    self.wrsb[:, :].rearrange("p (c f) -> p c f", c=8)[:, dc, :],
                    start=(dc == 0), stop=(dc == NDC - 1)),
                    wait=([tok_xn_all, self.tok_wr, tok_lg_evac] if dc == 0 else []),
                    inc="t" if dc == NDC - 1 else None)
            tlg = last
            V = "vector"
            prev = [None]

            def vstep(fn, extra=()):
                prev[0] = p.on(V, fn, wait=[prev[0]] + list(extra), inc="v")
                return prev[0]

            t1 = vstep(lambda e: e.tensor_copy(self.lt, self.pD[:, 0:8]),
                       extra=[tlg])
            tok_lg_evac = t1
            vstep(lambda e: e.reduce_max(self.v1c, self.lt, axis=AX.X))
            vstep(lambda e: e.tensor_single_scalar(self.eq1m, self.lt, self.v1c, ALU.is_ge))
            vstep(lambda e: e.scalar_tensor_tensor(
                self.ltmm, self.eq1m, -1e30, self.lt, ALU.mult, ALU.add))
            vstep(lambda e: e.reduce_max(self.v2c, self.ltmm, axis=AX.X))
            tv = vstep(lambda e: e.tensor_sub(self.dcol, self.v2c, self.v1c))
            te = p.on("scalar", lambda e: e.activation(self.edc, self.dcol, AF.Exp),
                      wait=[tv], inc="s")
            vstep(lambda e: e.tensor_single_scalar(self.denc, self.edc, 1.0, ALU.add),
                  extra=[te])
            vstep(lambda e: e.reciprocal(self.rw1c, self.denc))
            vstep(lambda e: e.tensor_mul(self.rw2c, self.edc, self.rw1c))
            vstep(lambda e: e.tensor_single_scalar(self.p1fc, self.rw1c, 1.0, ALU.is_ge))
            vstep(lambda e: e.tensor_single_scalar(self.p2fc, self.rw2c, 1.0, ALU.is_ge))
            vstep(lambda e: e.tensor_single_scalar(self.eq2m, self.ltmm, self.v2c, ALU.is_ge))
            vstep(lambda e: e.tensor_single_scalar(self.t2m, self.eq2m, self.p2fc, ALU.mult))
            tpr = vstep(lambda e: e.scalar_tensor_tensor(
                self.probm, self.eq1m, self.p1fc, self.t2m, ALU.mult, ALU.add),
                extra=[tok_probm_free])
            ttr = p.on("tensor", lambda e: e.transpose(
                self.pC[0:8, 0:128], self.probm, self.identf[:, :]),
                wait=[tpr, tok_tr_evac], inc="t")
            tok_probm_free = ttr
            tev = p.on("vector", lambda e, tc=tc: e.tensor_copy(
                self.prob_fm[:, 128 * tc:128 * (tc + 1)], self.pC[0:8, 0:128]),
                wait=[ttr], inc="v")
            tok_tr_evac = tev
            tok_probs.append(tev)
        tok_prob_sp = p.dma("sync", self.probsp_d, self.prob_fm, wait=[tok_probs[-1]])
        self.dbg(self.prob_fm, wait=[tok_probs[-1]])

        tok_w1 = {}
        tok_w2 = {}
        tok_he_T = {}
        tok_ys_T = {}
        tok_h2 = {}
        tok_pb = {}
        tok_gp_last = {}
        rp_free = [None, None]
        tok_x_upd = None
        tok_x_tq = [None] * NTQ
        un = [0]
        yn = [0]
        prev_pA_free = {}
        ys_free = [None] * 4

        for gi, g in enumerate(groups):
            for j, br in enumerate(g):
                w = [tok_he_T.get(gi - 1)]
                if gi == 0:
                    w = list(tok_arena_free)
                tok_w1[(gi, j)] = p.dma(
                    weng, w1v[j][:, :, :], w1_dram(br),
                    wait=list(w) + [tok_cache])
            for j, br in enumerate(g):
                if br != "s":
                    tok_pb[(gi, j)] = p.dma(
                        "sync", pb_slot[j],
                        self.probsp_d[br:br + 1, :].broadcast_to([128, T]),
                        wait=[tok_prob_sp, tok_gp_last.get(gi - 1)])
            for j, br in enumerate(g):
                for tq in range(NTQ):
                    for fcp in range(2):
                        u = un[0]; un[0] += 1
                        pp = [self.pAq[2 * (u % 2)], self.pAq[2 * (u % 2) + 1]]
                        lastmm = None
                        for dc in range(NDC):
                            for f2 in range(2):
                                fc = fcp * 2 + f2
                                lastmm = p.on("tensor", lambda e, j=j, dc=dc, fc=fc, tq=tq, ps=pp[f2]: e.matmul(
                                    ps, w1v[j][:, dc, 128 * fc:128 * (fc + 1)],
                                    xn(dc)[:, 512 * tq:512 * (tq + 1)],
                                    start=(dc == 0), stop=(dc == NDC - 1)),
                                    wait=([tok_w1[(gi, j)], tok_xn_all,
                                           prev_pA_free.get(u % 2)] if dc == 0 and f2 == 0 else []),
                                    inc="t" if (dc == NDC - 1 and f2 == 1) else None)
                        tok_he_T[gi] = lastmm
                        last_ev = None
                        for f2 in range(2):
                            fc = fcp * 2 + f2
                            re = h2s(j, fc, tq)
                            tre = p.on("scalar", lambda e, re=re, ps=pp[f2]: e.activation(
                                re, ps, AF.Relu), wait=[lastmm], inc="s")
                            last_ev = tre
                            if br == "s":
                                th = p.on("vector", lambda e, re=re: e.tensor_mul(
                                    re, re, re), wait=[tre], inc="v")
                            else:
                                rp = rps[f2]
                                tg_ = p.on("vector", lambda e, re=re, rp=rp, j=j, tq=tq: e.tensor_mul(
                                    rp, re, pb_slot[j][:, 512 * tq:512 * (tq + 1)]),
                                    wait=[tre, tok_pb[(gi, j)], rp_free[f2]], inc="v")
                                tok_gp_last[gi] = tg_
                                th = p.on("vector", lambda e, re=re, rp=rp: e.tensor_mul(
                                    re, rp, re), wait=[tg_], inc="v")
                                rp_free[f2] = th
                            tok_h2[(gi, tq)] = th
                        prev_pA_free[u % 2] = last_ev
            for j, br in enumerate(g):
                tok_w2[(gi, j)] = p.dma(
                    weng, w2v[j][:, :, :], w2_dram(br),
                    wait=[tok_ys_T.get(gi - 1), tok_cache] +
                         (list(tok_arena_free) if gi == 0 else []))
            for tq in range(NTQ):
                for do in range(NDC):
                    k = yn[0] % 4; yn[0] += 1
                    ps = self.ys_ring[k]
                    lastmm = None
                    nbr = len(g)
                    for j in range(nbr):
                        for fc in range(4):
                            first = (j == 0 and fc == 0)
                            lastmm = p.on("tensor", lambda e, j=j, fc=fc, do=do, tq=tq, ps=ps: e.matmul(
                                ps, w2v[j][:, fc, 128 * do:128 * (do + 1)],
                                h2s(j, fc, tq),
                                start=(j == 0 and fc == 0), stop=(j == nbr - 1 and fc == 3)),
                                wait=([tok_w2[(gi, 0)], tok_w2[(gi, nbr - 1)],
                                       tok_h2[(gi, tq)], ys_free[k]] if first else []),
                                inc="t" if (j == nbr - 1 and fc == 3) else None)
                    tok_ys_T[gi] = lastmm
                    tok_x_upd = p.on("vector", lambda e, do=do, tq=tq, ps=ps: e.scalar_tensor_tensor(
                        self.xr[:, 2048 * do + 512 * tq:2048 * do + 512 * (tq + 1)],
                        ps, 0.5,
                        self.xr[:, 2048 * do + 512 * tq:2048 * do + 512 * (tq + 1)],
                        ALU.mult, ALU.add),
                        wait=[lastmm, tok_xres], inc="v")
                    ys_free[k] = tok_x_upd
                    if gi == len(groups) - 1 and do == NDC - 1:
                        tok_x_tq[tq] = tok_x_upd
        return tok_x_upd, tok_ys_T[len(groups) - 1], tok_x_tq

    def qkv_phase(self, xn, tok_xn_all, tok_x_spill, tok_gp_done):
        p = self
        wslot = [self.AR4[:, 0:8192].rearrange("p (c f) -> p c f", c=8),
                 self.AR4[:, 8192:16384].rearrange("p (c f) -> p c f", c=8)]
        qv = self.AR1[:, 0:16384]
        kv = self.AR1[:, 16384:32768]
        tok_bq8 = p.on("vector", lambda e: e.tensor_scalar_mul(
            self.bq8c, self.bq8c, 0.125), wait=[self.tok_bq], inc="v")
        bvb = self.AR5[:, 0:2048].bitcast(F32)
        tok_bvb = p.dma("sync", bvb, self.bv_d.broadcast_to([128, D]),
                        wait=[self.tok_bvb_free])
        tok_wq = p.dma("gpsimd", wslot[0][:, :, :],
                       self.Wq_d.rearrange("(c p) f -> p c f", p=128),
                       wait=[self.tok_w_arena_free])
        tok_wk = p.dma("gpsimd", wslot[1][:, :, :],
                       self.Wk_d.rearrange("(c p) f -> p c f", p=128),
                       wait=[self.tok_w_arena_free])
        ring = [self.pAq[0], self.pAq[1], self.pAq[2], self.pAq[3]]
        rfree = [self.tok_psum_free] * 4
        un = [0]

        def proj(wv_, tok_w, dest, bias_col, scale, extra_w=()):
            toks = []
            for do in range(NDC):
                for tp in range(NTQ):
                    k = un[0] % 4; un[0] += 1
                    ps = ring[k]
                    lastmm = None
                    for dc in range(NDC):
                        lastmm = p.on("tensor", lambda e, dc=dc, do=do, tp=tp, ps=ps, wv_=wv_: e.matmul(
                            ps, wv_[:, dc, 128 * do:128 * (do + 1)],
                            xn(dc)[:, 512 * tp:512 * (tp + 1)],
                            start=(dc == 0), stop=(dc == NDC - 1)),
                            wait=([tok_w, tok_xn_all, rfree[k]] + list(extra_w)
                                  if dc == 0 else []),
                            inc="t" if dc == NDC - 1 else None)
                    tev = p.on("scalar", lambda e, ps=ps, do=do, tp=tp, dest=dest,
                               bias_col=bias_col, scale=scale: e.activation(
                                   dest[:, 2048 * do + 512 * tp:2048 * do + 512 * (tp + 1)],
                                   ps, AF.Identity, bias=bias_col(do), scale=scale),
                               wait=[lastmm], inc="s")
                    rfree[k] = tev
                    toks.append(tev)
            return toks, lastmm

        qtoks, tok_q_T = proj(wslot[0], tok_wq, qv,
                              lambda do: self.bq8c[:, do:do + 1], 0.125,
                              extra_w=[tok_x_spill, tok_gp_done, tok_bq8])
        ktoks, tok_k_T = proj(wslot[1], tok_wk, kv,
                              lambda do: self.pcols[:, 0 + do:1 + do], 1.0,
                              extra_w=[tok_x_spill, tok_gp_done, self.tok_pcols])
        tok_wv = p.dma("gpsimd", wslot[0][:, :, :],
                       self.Wv_d.rearrange("(c p) f -> p c f", p=128),
                       wait=[tok_q_T])
        vaug = self.AR3[:, 0:16640]
        tok_v = []
        vfree = {0: self.tok_psum_free, 1: self.tok_psum_free}
        for tc in range(NTC):
            ps = [self.pBq[0], self.pBq[1]] if tc % 2 == 0 else [self.pC, self.pD]
            lastmm = None
            for dc in range(NDC):
                for hh in range(2):
                    lastmm = p.on("tensor", lambda e, dc=dc, tc=tc, hh=hh, ps=ps: e.matmul(
                        ps[hh], xn(dc)[:, 128 * tc:128 * (tc + 1)],
                        wslot[0][:, dc, 512 * hh:512 * (hh + 1)],
                        start=(dc == 0), stop=(dc == NDC - 1)),
                        wait=([tok_wv, vfree[tc % 2]] if dc == 0 and hh == 0 else []),
                        inc="t" if (dc == NDC - 1 and hh == 1) else None)
            dst = vaug[:, 1040 * tc:1040 * (tc + 1)].rearrange("p (h x) -> p h x", h=16)
            tvv = None
            for hh in range(2):
                src = ps[hh].rearrange("p (h x) -> p h x", h=8)
                bvv = bvb[:, 512 * hh:512 * (hh + 1)].rearrange("p (h x) -> p h x", h=8)
                tvv = p.on("vector", lambda e, src=src, dst=dst, hh=hh, bvv=bvv: e.tensor_tensor(
                    dst[:, 8 * hh:8 * (hh + 1), 0:64], src, bvv, ALU.add),
                    wait=[lastmm, tok_bvb, self.tok_varena_free if tc == 0 and hh == 0 else None],
                    inc="v")
            tm = p.on("vector", lambda e, dst=dst: e.memset(dst[:, :, 64:65], 1.0),
                      wait=[tvv], inc="v")
            vfree[tc % 2] = tm
            tok_v.append(tm)
        self.tok_qkv = (qtoks[-1], ktoks[-1], tok_v[-1])
        self.tok_qkv_T_last = lastmm
        return qtoks[-1], ktoks[-1], tok_v[-1]

    def attn_phase(self, tok_q, tok_k, tok_v):
        p = self
        kv_off = 16384
        wide = self.AR6[:, 0:7936].bitcast(F32)
        widb = self.AR6[:, 0:3968]
        t_wa = p.dma("sync", wide, self.widea_d, wait=[self.tok_wide_arena_free])
        acol = self.sm[:, 96:97]
        t_ds = p.dma("sync", acol, self.ds_d.broadcast_to([128, 1]))
        t_a = p.on("vector", lambda e: e.tensor_scalar_mul(acol, acol, -0.125),
                   wait=[t_ds], inc="v")
        t_b = p.on("vector", lambda e: e.tensor_single_scalar(
            wide, wide, acol, ALU.mult), wait=[t_wa, t_a], inc="v")
        mbcol = self.sm[:, 100:101]
        t_mb = p.on("vector", lambda e: e.reduce_max(mbcol, wide, axis=AX.X),
                    wait=[t_b], inc="v")
        t_tr = p.on("tensor", lambda e: e.transpose(
            self.pD[0:1, 0:128], mbcol, self.identf[:, :]),
            wait=[t_mb, self.tok_psum_free], inc="t")
        mbrow = self.row_slot[:, 0:128]
        t_cp = p.on("vector", lambda e: e.tensor_copy(mbrow, self.pD[0:1, 0:128]),
                    wait=[t_tr], inc="v")
        mbs = self.sm[0:1, 104:105]
        t_rd = p.on("vector", lambda e: e.reduce_max(mbs, mbrow, axis=AX.X,
                                                     negate=True),
                    wait=[t_cp], inc="v")
        t_bc = p.on("tensor", lambda e: e.matmul(
            self.pD[:, 0:1], self.onesr[:, :], mbs, start=True, stop=True),
            wait=[t_rd], inc="t")
        nmb = self.sm[:, 108:109]
        t_nmb = p.on("vector", lambda e: e.tensor_copy(nmb, self.pD[:, 0:1]),
                     wait=[t_bc], inc="v")
        t_dp = p.on("scalar", lambda e: e.activation(widb, wide, AF.Exp,
                                                     bias=nmb, scale=1.0),
                    wait=[t_nmb], inc="s")
        self.tok_dprime = t_dp

        slots = [self.pA[:, 0:1024], self.pA[:, 1024:2048], self.pB[:, 0:1024]]
        accs = [self.pB[:, 1024:1536], self.pB[:, 1536:2048]]
        ptb = [self.AR5[:, 1024 * k:1024 * (k + 1)] for k in range(3)]
        ebuf = [self.AR5[:, 3072 + 1024 * k:3072 + 1024 * (k + 1)] for k in range(3)]
        linv = self.AR5[:, 6144:7168].bitcast(F32)
        lrow = self.row_slot[:, 0:512]

        def head_ap(base, h, lo, sz):
            po = 64 * (h % 2)
            dc = h // 2
            return self.AR1[po:po + 64, base + 2048 * dc + lo:base + 2048 * dc + lo + sz]

        slot_free = [self.tok_psum_free] * 3
        e_free = [None] * 3
        pt_free = [None] * 3
        acc_free = [self.tok_psum_free, t_nmb]
        tok_a_evac = None
        tok_last_T = None
        LAG = 2

        for pair in range(8):
            hA, hB = 2 * pair, 2 * pair + 1
            for tb in range(4):
                tok_pv = None
                tv_done = [None] * NTC

                def emit_pv(uc):
                    last = None
                    for hh, h in enumerate((hA, hB)):
                        last = p.on("tensor", lambda e, uc=uc, hh=hh, h=h: e.matmul(
                            accs[hh][0:65, :],
                            self.av_slice(uc, h),
                            ptb[uc % 3][:, 512 * hh:512 * (hh + 1)],
                            start=(uc == 0), stop=(uc == NTC - 1)),
                            wait=[tv_done[uc], tok_v] +
                                 ([acc_free[hh]] if uc == 0 else []),
                            inc="t" if hh == 1 else None)
                    return last

                for uc in range(NTC):
                    sl = uc % 3
                    po = slots[sl]
                    lastmm = None
                    for hh, h in enumerate((hA, hB)):
                        lastmm = p.on("tensor", lambda e, h=h, po=po, hh=hh, uc=uc, tb=tb: e.matmul(
                            po[:, 512 * hh:512 * (hh + 1)],
                            head_ap(kv_off, h, 128 * uc, 128),
                            head_ap(0, h, 512 * tb, 512),
                            start=True, stop=True,
                            tile_position=(64 * (h % 2), 0)),
                            wait=([slot_free[sl], tok_q, tok_k, self.tok_dprime]
                                  if hh == 0 else []),
                            inc="t" if hh == 1 else None)
                    te = p.on("scalar", lambda e, po=po, sl=sl: e.activation(
                        ebuf[sl], po, AF.Exp, bias=-12.0, scale=1.0),
                        wait=[lastmm, e_free[sl]], inc="s")
                    slot_free[sl] = te
                    off = 1920 + 512 * tb - 128 * uc
                    tv = None
                    for hh in range(2):
                        tv = p.on("vector", lambda e, sl=sl, off=off, hh=hh: e.tensor_tensor(
                            ptb[sl][:, 512 * hh:512 * (hh + 1)],
                            ebuf[sl][:, 512 * hh:512 * (hh + 1)],
                            widb[:, off:off + 512], ALU.mult),
                            wait=[te, pt_free[sl]] if hh == 0 else [],
                            inc="v" if hh == 1 else None)
                    if uc == 0:
                        if hA > 0:
                            tv = p.on("vector", lambda e, sl=sl, hA=hA: e.memset(
                                ptb[sl][0:hA, 0:512], 0.0), wait=[tv], inc="v")
                        tv = p.on("vector", lambda e, sl=sl, hB=hB: e.memset(
                            ptb[sl][0:hB, 512:1024], 0.0), wait=[tv], inc="v")
                    tv_done[uc] = tv
                    e_free[sl] = tv
                    if uc >= 2:
                        tok_pv = emit_pv(uc - 2)
                        pt_free[(uc - 2) % 3] = tok_pv
                for uc in range(NTC - 2, NTC):
                    tok_pv = emit_pv(uc)
                    pt_free[uc % 3] = tok_pv
                tok_last_T = tok_pv
                for hh, h in enumerate((hA, hB)):
                    ps = accs[hh]
                    t1 = p.on("vector", lambda e, ps=ps: e.tensor_single_scalar(
                        lrow, ps[64:65, :], 1e-30, ALU.add),
                        wait=[tok_pv], inc="v")
                    t2a = p.on("scalar", lambda e: e.activation(
                        lrow, lrow, AF.Ln), wait=[t1], inc="s")
                    t2 = p.on("scalar", lambda e: e.activation(
                        lrow, lrow, AF.Exp, bias=0.0, scale=-1.0),
                        wait=[t2a], inc="s")
                    t3 = p.on("tensor", lambda e: e.matmul(
                        self.pA[0:64, 0:512], self.onesr[0:1, 0:64], lrow,
                        start=True, stop=True),
                        wait=[t2, slot_free[0]], inc="t")
                    t4 = p.on("vector", lambda e: e.tensor_copy(
                        linv[0:64, :], self.pA[0:64, 0:512]),
                        wait=[t3], inc="v")
                    slot_free[0] = t4
                    dc = h // 2
                    po_ = 64 * (h % 2)
                    dst = self.AR2[po_:po_ + 64,
                                   2048 * dc + 512 * tb:2048 * dc + 512 * (tb + 1)]
                    t5 = p.on("vector", lambda e, ps=ps, dst=dst: e.tensor_tensor(
                        dst, ps[0:64, :], linv[0:64, :], ALU.mult),
                        wait=[t4, self.tok_a_arena_free
                              if (pair == 0 and tb == 0) else None],
                        inc="v")
                    acc_free[hh] = t5
                    tok_a_evac = t5
        self.tok_attn_A = tok_a_evac
        self.tok_attn_T = tok_last_T
        self.tok_pv_free_final = list(acc_free) + list(acc_free)
        self.tok_pa_free_final = list(slot_free)
        return tok_a_evac

    def av_slice(self, uc, h):
        return self.AR3[:, 1040 * uc + 65 * h:1040 * uc + 65 * (h + 1)]

    def wo_ln1_phase(self):
        p = self
        wslot = self.AR4[:, 0:8192].rearrange("p (c f) -> p c f", c=8)
        tok_wo = p.dma("gpsimd", wslot[:, :, :],
                       self.Wo_d.rearrange("(c p) f -> p c f", p=128),
                       wait=[self.tok_qkv_T_last])
        mu2b = self.AR3[:, 0:4096].bitcast(F32)
        r2b = self.AR3[:, 4096:8192].bitcast(F32)
        row = self.row_slot
        tok_tiles = []
        tpsg = list(self.tok_pa_free_final) + list(self.tok_pv_free_final)
        for i, dstt in enumerate((mu2b, r2b)):
            trr = p.dma("sync", row, self.rowsp_d[:, T * i:T * (i + 1)],
                        wait=[self.tok_attn_A] + (tok_tiles[-4:] if i else []))
            for q in range(4):
                tb = p.on("tensor", lambda e, q=q: e.matmul(
                    self.pAq[q], self.onesr[:, :], row[:, 512 * q:512 * (q + 1)],
                    start=True, stop=True),
                    wait=[trr] + (tpsg if i == 0 and q == 0 else []), inc="t")
                te = p.on("scalar", lambda e, q=q, dstt=dstt: e.copy(
                    dstt[:, 512 * q:512 * (q + 1)], self.pAq[q]),
                    wait=[tb], inc="s")
                tok_tiles.append(te)
        x1r = [self.AR6[:, 0:4096].bitcast(F32), self.AR6[:, 4096:8192].bitcast(F32)]
        tsc = [self.AR5[:, 0:1024].bitcast(F32), self.AR5[:, 1024:2048].bitcast(F32)]
        usc = [self.AR5[:, 4096:5120].bitcast(F32), self.AR5[:, 5120:6144].bitcast(F32)]
        tok_x1 = [None] * NDC
        tok_R = []
        ring = [self.pAq[0], self.pAq[1], self.pAq[2], self.pAq[3]]
        rfree = [tok_tiles[-1]] * 4
        tfree = [None, None]
        ufree = [None, None]
        un = [0]
        last_wo_T = None
        for do in range(NDC):
            tok_x1[do] = p.dma(
                "sync", x1r[do % 2],
                self.x1sp_d[:, 2048 * do:2048 * (do + 1)],
                wait=[self.tok_x1_spilled, self.tok_attn_T,
                      tok_R[-5] if len(tok_R) >= 5 else None])
            for tp in range(NTQ):
                k = un[0] % 4; un[0] += 1
                ps = ring[k]
                lastmm = None
                for dc in range(NDC):
                    lastmm = p.on("tensor", lambda e, dc=dc, do=do, tp=tp, ps=ps: e.matmul(
                        ps, wslot[:, dc, 128 * do:128 * (do + 1)],
                        self.AR2[:, 2048 * dc + 512 * tp:2048 * dc + 512 * (tp + 1)],
                        start=(dc == 0), stop=(dc == NDC - 1)),
                        wait=([tok_wo, self.tok_attn_A, rfree[k]] if dc == 0 else []),
                        inc="t" if dc == NDC - 1 else None)
                last_wo_T = lastmm
                tg = p.on("gpsimd", lambda e, do=do, tp=tp: e.tensor_sub(
                    tsc[tp % 2], x1r[do % 2][:, 512 * tp:512 * (tp + 1)],
                    mu2b[:, 512 * tp:512 * (tp + 1)]),
                    wait=[tok_x1[do], tok_tiles[-1], tfree[tp % 2]], inc="g")
                tu = p.on("vector", lambda e, tp=tp: e.tensor_mul(
                    usc[tp % 2], tsc[tp % 2], r2b[:, 512 * tp:512 * (tp + 1)]),
                    wait=[tg, ufree[tp % 2]], inc="v")
                tfree[tp % 2] = tu
                tR = p.on("vector", lambda e, ps=ps, do=do, tp=tp: e.scalar_tensor_tensor(
                    self.xr[:, 2048 * do + 512 * tp:2048 * do + 512 * (tp + 1)],
                    ps, self.pcols[:, 8 + do:9 + do], usc[tp % 2],
                    ALU.add, ALU.add),
                    wait=[lastmm, tu],
                    inc="v")
                ufree[tp % 2] = tR
                rfree[k] = tR
                tok_R.append(tR)
        self.tok_R = tok_R[-1]
        self.tok_wo_T = last_wo_T
        mub = self.AR5[:, 0:4096].bitcast(F32)
        rb = self.AR5[:, 4096:8192].bitcast(F32)
        xbf_ring = [self.AR6[:, 0:2048], self.AR6[:, 2048:4096]]
        tsc_ring = [self.AR6[:, 0:4096].bitcast(F32),
                    self.AR6[:, 4096:8192].bitcast(F32)]
        xfn = lambda dc: self.xr[:, 2048 * dc:2048 * (dc + 1)]
        xafn = lambda dc: self.AR2[:, 2048 * dc:2048 * (dc + 1)]
        tok_xa, tok_xa_all, tok_gp = self.ln(
            xfn, xafn, self.tok_R, mub, rb, xbf_ring, tsc_ring,
            gamma_col=lambda dc: self.pcols[:, 56 + dc:57 + dc],
            beta_col=lambda dc: self.pcols[:, 64 + dc:65 + dc],
            extra_wait=[self.tok_R],
            psum_guard=[self.tok_R],
            apply_wait=[last_wo_T],
            row_wait=[tok_tiles[-1]])
        self.tok_xa = tok_xa
        self.tok_ln1_gp = tok_gp
        return tok_xa

    def ffn_ln2_phase(self):
        p = self
        w1s = [self.AR4[:, 0:4096].rearrange("p (c f) -> p c f", c=8),
               self.AR4[:, 4096:8192].rearrange("p (c f) -> p c f", c=8)]
        w2s = [self.AR4[:, 8192:12288].rearrange("p (c f) -> p c f", c=32),
               self.AR4[:, 12288:16384].rearrange("p (c f) -> p c f", c=32)]
        h2 = self.AR3[:, 0:16384]
        R2 = self.AR5[:, 0:8192].bitcast(F32)
        xbf = [self.AR6[:, 0:512], self.AR6[:, 512:1024]]
        sqr = [self.AR6[:, 1024:1536], self.AR6[:, 1536:2048]]
        x1r = [self.AR6[:, 2048:3072].bitcast(F32), self.AR6[:, 3072:4096].bitcast(F32)]
        zsc = [self.AR6[:, 4096:5120].bitcast(F32), self.AR6[:, 5120:6144].bitcast(F32)]
        tscr = [self.AR6[:, 6144:7168].bitcast(F32), self.AR6[:, 7168:8192].bitcast(F32)]
        mub2 = self.AR7[:, 0:1024].bitcast(F32)
        rb2 = self.AR7[:, 1024:2048].bitcast(F32)
        rowq = self.AR7[0:1, 2048:3072].bitcast(F32)

        tok_w1 = [None, None]
        tok_w2 = [None, None]
        w1_free = [self.tok_wo_T, self.tok_wo_T]
        w2_free = [self.tok_wo_T, self.tok_wo_T]
        pa_free = {0: self.tok_xa, 1: self.tok_xa}
        ys_free = [self.tok_xa] * 4
        r2_free = None
        tok_x2 = None
        xa = lambda dc: self.AR2[:, 2048 * dc:2048 * (dc + 1)]
        un = [0]
        x1_free = [self.tok_xa, self.tok_xa]
        tok_ln2_prev = None
        for tq in range(NTQ):
            tok_h2 = None
            for fg in range(8):
                j = fg % 2
                tok_w1[j] = p.dma("sync", w1s[j][:, :, :],
                                  self.wc_W1a_d[:, 512 * fg:512 * (fg + 1)].rearrange(
                                      "(c p) f -> p c f", p=128),
                                  wait=[w1_free[j], self.tok_wc_ffn])
                for fcp in range(2):
                    u = un[0] % 2; un[0] += 1
                    pp = [self.pAq[2 * u], self.pAq[2 * u + 1]]
                    lastmm = None
                    for dc in range(NDC):
                        for f2 in range(2):
                            fc = fcp * 2 + f2
                            lastmm = p.on("tensor", lambda e, j=j, dc=dc, fc=fc, tq=tq, ps=pp[f2]: e.matmul(
                                ps, w1s[j][:, dc, 128 * fc:128 * (fc + 1)],
                                xa(dc)[:, 512 * tq:512 * (tq + 1)],
                                start=(dc == 0), stop=(dc == NDC - 1)),
                                wait=([tok_w1[j], self.tok_xa, pa_free[u]]
                                      if dc == 0 and f2 == 0 else []),
                                inc="t" if (dc == NDC - 1 and f2 == 1) else None)
                    w1_free[j] = lastmm
                    last_ev = None
                    for f2 in range(2):
                        fc = fcp * 2 + f2
                        gfc = fg * 4 + fc
                        last_ev = p.on("scalar", lambda e, gfc=gfc, ps=pp[f2]: e.activation(
                            h2[:, 512 * gfc:512 * (gfc + 1)], ps, AF.Relu,
                            bias=self.pcols[:, 16 + gfc:17 + gfc], scale=1.0),
                            wait=[lastmm, self.tok_pcols,
                                  self.tok_h2_dead if tq == 0 and gfc == 0 else None],
                            inc="s")
                        tok_h2 = last_ev
                    pa_free[u] = last_ev
            tok_R2 = []
            for do in range(NDC):
                j = do % 2
                tok_w2[j] = p.dma("sync", w2s[j][:, :, :],
                                  self.wc_W2a_d[:, 128 * do:128 * (do + 1)].rearrange(
                                      "(c p) f -> p c f", p=128),
                                  wait=[w2_free[j], self.tok_wc_ffn])
                k = (do + 4 * tq) % 4
                ps = self.ys_ring[k]
                lastmm = None
                for fc in range(32):
                    lastmm = p.on("tensor", lambda e, fc=fc, do=do, j=j, ps=ps: e.matmul(
                        ps, w2s[j][:, fc, :],
                        h2[:, 512 * fc:512 * (fc + 1)],
                        start=(fc == 0), stop=(fc == 31)),
                        wait=([tok_w2[j], tok_h2, ys_free[k]] if fc == 0 else []),
                        inc="t" if fc == 31 else None)
                w2_free[j] = lastmm
                tR2 = p.on("vector", lambda e, ps=ps, do=do, tq=tq: e.scalar_tensor_tensor(
                    R2[:, 512 * do:512 * (do + 1)], ps,
                    self.pcols[:, 48 + do:49 + do],
                    xa(do)[:, 512 * tq:512 * (tq + 1)], ALU.add, ALU.add),
                    wait=[lastmm, r2_free, self.tok_pcols], inc="v")
                ys_free[k] = tR2
                tok_R2.append(tR2)
            tok_cast = [None] * NDC
            tok_mm = [None] * NDC
            for dc in range(NDC):
                xb = xbf[dc % 2]
                tok_cast[dc] = p.on("scalar", lambda e, xb=xb, dc=dc: e.copy(
                    xb, R2[:, 512 * dc:512 * (dc + 1)]),
                    wait=[tok_R2[dc], tok_mm[dc - 2] if dc >= 2 else None], inc="s")
                sq = sqr[dc % 2]
                tsq = p.on("vector", lambda e, xb=xb, sq=sq: e.tensor_mul(sq, xb, xb),
                           wait=[tok_cast[dc]], inc="v")
                p.on("tensor", lambda e, xb=xb, dc=dc: e.matmul(
                    self.pAq[0][0:1, :], self.onescb[:, :], xb,
                    start=(dc == 0), stop=(dc == NDC - 1)),
                    wait=[tok_cast[dc]] + ([pa_free[0], pa_free[1]] if dc == 0 else []))
                tok_mm[dc] = p.on("tensor", lambda e, sq=sq, dc=dc: e.matmul(
                    self.pAq[1][0:1, :], self.onescb[:, :], sq,
                    start=(dc == 0), stop=(dc == NDC - 1)),
                    wait=[tsq], inc="t")
            tok_sums = tok_mm[NDC - 1]
            rowq2 = self.AR7[0:1, 3072:4096].bitcast(F32)
            t1 = p.on("vector", lambda e: e.tensor_copy(rowq, self.pAq[0][0:1, :]),
                      wait=[tok_sums], inc="v")
            t1b = p.on("vector", lambda e, rowq2=rowq2: e.tensor_copy(
                rowq2, self.pAq[1][0:1, :]), wait=[t1], inc="v")
            t2 = p.on("vector", lambda e: e.tensor_scalar_mul(rowq, rowq, 1.0 / D),
                      wait=[t1b], inc="v")
            tb1 = p.on("tensor", lambda e: e.matmul(
                self.pAq[0], self.onesr[:, :], rowq, start=True, stop=True),
                wait=[t2], inc="t")
            te1 = p.on("scalar", lambda e: e.copy(mub2[:, 0:512], self.pAq[0]),
                       wait=[tb1], inc="s")
            t3 = p.on("vector", lambda e: e.tensor_mul(rowq, rowq, rowq),
                      wait=[tb1], inc="v")
            t5 = p.on("vector", lambda e, rowq2=rowq2: e.scalar_tensor_tensor(
                rowq2, rowq2, 1.0 / D, rowq, ALU.mult, ALU.subtract),
                wait=[t3], inc="v")
            t6 = p.on("scalar", lambda e, rowq2=rowq2: e.activation(
                rowq2, rowq2, AF.Ln, bias=LN_EPS, scale=1.0),
                wait=[t5], inc="s")
            t7 = p.on("scalar", lambda e, rowq2=rowq2: e.activation(
                rowq2, rowq2, AF.Exp, bias=0.0, scale=-0.5),
                wait=[t6], inc="s")
            tb2 = p.on("tensor", lambda e, rowq2=rowq2: e.matmul(
                self.pAq[1], self.onesr[:, :], rowq2, start=True, stop=True),
                wait=[t7], inc="t")
            te2 = p.on("scalar", lambda e: e.copy(rb2[:, 0:512], self.pAq[1]),
                       wait=[tb2], inc="s")
            pa_free = {0: te2, 1: pa_free[1]}
            un[0] += (un[0] + 1) % 2
            tok_ln2_prev = None
            for dc in range(NDC):
                tok_x1q = p.dma("sync", x1r[dc % 2],
                                self.x1sp_d[:, 2048 * dc + 512 * tq:
                                            2048 * dc + 512 * (tq + 1)],
                                wait=[x1_free[dc % 2]])
                ts_ = tscr[dc % 2]
                tg = p.on("gpsimd", lambda e, ts_=ts_, dc=dc: e.tensor_sub(
                    ts_, R2[:, 512 * dc:512 * (dc + 1)], mub2[:, 0:512]),
                    wait=[te1, te2, tok_ln2_prev], inc="g")
                tu = p.on("vector", lambda e, ts_=ts_: e.tensor_mul(
                    ts_, ts_, rb2[:, 0:512]), wait=[tg], inc="v")
                z = zsc[dc % 2]
                tz = p.on("scalar", lambda e, ts_=ts_, z=z, dc=dc: e.activation(
                    z, ts_, AF.Identity, bias=self.pcols[:, 80 + dc:81 + dc],
                    scale=self.pcols[:, 72 + dc:73 + dc]),
                    wait=[tu], inc="s")
                tok_x2 = p.on("vector", lambda e, z=z, dc=dc, tq=tq: e.tensor_add(
                    self.xr[:, 2048 * dc + 512 * tq:2048 * dc + 512 * (tq + 1)],
                    z, x1r[dc % 2]),
                    wait=[tz, tok_x1q, self.tok_R], inc="v")
                x1_free[dc % 2] = tok_x2
                tok_ln2_prev = tok_x2
            r2_free = tok_x2
        self.tok_x2 = tok_x2
        self.tok_ffn_T_list = [w1_free[0], w1_free[1]]
        return tok_x2

    def phase_out(self, tok_x_final, tok_psum_free=(), tok_per_tq=None):
        p = self
        oring = [self.AR6[:, 0:2048].bitcast(F32),
                 self.AR6[:, 2048:4096].bitcast(F32)]
        tok_out_dma = [None] * NTC
        prev_ev = {}
        for tc in range(NTC):
            slot = oring[tc % 2]
            last_ev = None
            tok_x = tok_per_tq[tc // 4] if tok_per_tq else tok_x_final
            for dc in range(NDC):
                widx = tc * NDC + dc
                ps = self.pD[:, 0:128] if widx % 2 == 0 else self.pC[:, 0:128]
                tt = p.on("tensor", lambda e, ps=ps, dc=dc, tc=tc: e.transpose(
                    ps, self.xr[:, 2048 * dc + 128 * tc:2048 * dc + 128 * (tc + 1)],
                    self.identf[:, :]),
                    wait=[tok_x, prev_ev.get(widx - 2),
                          tok_out_dma[tc - 2] if tc >= 2 else None] +
                         (list(tok_psum_free) if widx == 0 else []),
                    inc="t")
                last_ev = p.on("vector", lambda e, ps=ps, dc=dc, slot=slot: e.tensor_copy(
                    slot[:, 128 * dc:128 * (dc + 1)], ps),
                    wait=[tt], inc="v")
                prev_ev[widx] = last_ev
            tok_out_dma[tc] = p.dma(
                "sync", self.out_d[128 * tc:128 * (tc + 1), :], slot,
                wait=[last_ev])

    def build(self):
        nc = self.nc
        self.declare()
        import contextlib
        es = contextlib.ExitStack()
        sb = lambda name, shape, dtype: es.enter_context(nc.sbuf_tensor(name, shape, dtype))
        self.AR1 = sb("AR1", [128, 2 * NDC * T], BF16)
        self.AR2 = sb("AR2", [128, NDC * T], BF16)
        self.AR3 = sb("AR3", [128, 18432], BF16)
        self.AR4 = sb("AR4", [128, NDC * T], BF16)
        self.AR5 = sb("AR5", [128, 8192], BF16)
        self.AR6 = sb("AR6", [128, 8192], BF16)
        self.AR7 = sb("AR7", [128, 4608], BF16)
        self.identf = sb("identfs", [128, 128], F32)
        self.onesr = sb("onesrs", [1, 128], F32)
        self.onesrb = sb("onesrbs", [1, 128], BF16)
        self.onescb = sb("onescbs", [128, 1], BF16)
        self.wrsb = sb("wrsbs", [128, 64], BF16)
        self.pcols = sb("pcolss", [128, 96], F32)[:, :]
        self.pA = es.enter_context(nc.psum_tensor("pA", [128, 2048], F32))
        self.pB = es.enter_context(nc.psum_tensor("pB", [128, 2048], F32))
        self.pC = self.pB[:, 1024:1536]
        self.pD = self.pB[:, 1536:2048]
        sems = {}
        for s in ("t", "v", "s", "g", "ds", "dg", "dsc"):
            sems[s] = es.enter_context(nc.semaphore("sem_" + s))
        block = es.enter_context(nc.Block())

        self.views()

        self.tok_consts = None
        for i, cval in enumerate([LN_EPS, -0.5, 0.125, -12.0, -1.0]):
            col = self.sm[:, 66 + i:67 + i]
            self.tok_consts = self.on(
                "vector", lambda e, col=col, cval=cval: e.memset(col, cval),
                inc="v")
            nc.const_aps.register(F32, cval, col)

        self.tok_ident = self.dma("sync", self.identf[:, :], self.identf_d)
        self.tok_onesr = self.dma("sync", self.onesr[:, :], self.onesr_d)
        self.tok_onesrb = self.dma("sync", self.onesrb[:, :], self.onesrb_d)
        self.tok_onescb = self.dma("sync", self.onescb[:, :], self.onescb_d)
        self.tok_wr = self.dma("gpsimd",
                               self.wrsb[:, :].rearrange("p (c f) -> p c f", c=8),
                               self.Wr_d.rearrange("(c p) f -> p c f", p=128))
        self.tok_bq = self.dma(
            "sync", self.bq8c, self.bq_d.rearrange("o (c p) -> p (o c)", p=128),
            slow=True)
        pc = self.pcols
        tpc = None
        for dst_lo, src in ((0, self.bk_d), (8, self.bo_d), (16, self.b1a_d),
                            (48, self.b2a_d), (56, self.g1_d), (64, self.be1_d),
                            (72, self.g2_d), (80, self.be2_d)):
            w_ = src.shape[1] // 128
            tpc = self.dma("sync", pc[:, dst_lo:dst_lo + w_],
                           src.rearrange("o (c p) -> p (o c)", p=128), slow=True)
        self.tok_pcols_l = tpc

        self.phase_in()

        mub = self.AR5[:, 0:4096].bitcast(F32)
        rb = self.AR5[:, 4096:8192].bitcast(F32)
        xbf_ring = [self.AR6[:, 0:2048], self.AR6[:, 2048:4096]]
        self.sq_ring = [self.AR6[:, 4096:6144], self.AR6[:, 6144:8192]]
        tsc_ring = [self.AR6[:, 0:4096].bitcast(F32),
                    self.AR6[:, 4096:8192].bitcast(F32)]
        xfn = lambda dc: self.xr[:, 2048 * dc:2048 * (dc + 1)]
        xnfn = lambda dc: self.AR2[:, 2048 * dc:2048 * (dc + 1)]
        tok_xn_last, tok_xn_all_l, tok_ln_gp = self.ln(
            xfn, xnfn, self.tok_x_ready, mub, rb, xbf_ring, tsc_ring,
            extra_wait=[self.tok_onescb, self.tok_onesr, self.tok_in_trans_done])
        self.dbg(self.AR2[:, :], wait=[tok_xn_last])
        self.dbg(mub[:, :], wait=[tok_xn_last])
        self.dbg(rb[:, :], wait=[tok_xn_last])
        tok_moe_x, tok_moe_T, tok_moe_tq = self.moe(
            xnfn, tok_xn_last, tok_xn_last, tok_arena_free=(), tag=0,
            tok_xres=self.tok_x_ready)

        tok_x_final = tok_moe_x
        tok_out_psum_guard = [tok_moe_T]
        if STAGE >= 2:
            tok_xn2, tok_xn2_all, tok_lnb_gp = self.ln(
                xfn, xnfn, tok_moe_x, mub, rb, xbf_ring, tsc_ring,
                spill=True,
                extra_wait=[tok_moe_x],
                psum_guard=[tok_moe_x],
                row_wait=[])
            self.tok_x1_spilled = self.dma("sync", self.x1sp_d, self.xr,
                                           wait=[tok_moe_x])
            self.tok_w_arena_free = tok_moe_T
            self.tok_psum_free = tok_moe_x
            self.tok_varena_free = tok_moe_T
            self.tok_bvb_free = tok_lnb_gp
            self.tok_wide_arena_free = tok_xn2
            self.tok_pcols = self.tok_pcols_l
            tq_, tk_, tv_ = self.qkv_phase(
                xnfn, tok_xn2, self.tok_x1_spilled, tok_lnb_gp)
            self.tok_wc_ffn = None
            for dst, src in ((self.wc_W1a_d, self.W1a_d),
                             (self.wc_W2a_d, self.W2a_d)):
                self.tok_wc_ffn = self.dma("gpsimd", dst, src)
            self.tok_wc_moe = None
            for dst, src in ((self.wc_W1e_d, self.W1e_d),
                             (self.wc_W2e_d, self.W2e_d),
                             (self.wc_Ws1_d, self.Ws1_d),
                             (self.wc_Ws2_d, self.Ws2_d)):
                self.tok_wc_moe = self.dma("gpsimd", dst, src)
            self.tok_a_arena_free = self.tok_qkv_T_last
            self.attn_phase(tq_, tk_, tv_)
            self.wo_ln1_phase()
            self.tok_h2_dead = self.tok_R
            self.ffn_ln2_phase()
            tok_x_final = self.tok_x2
            tok_out_psum_guard = [self.tok_x2]

        if STAGE >= 3:
            tok_xn3, _, tok_lnc_gp = self.ln(
                xfn, xnfn, self.tok_x2, mub, rb, xbf_ring, tsc_ring,
                extra_wait=[self.tok_x2],
                psum_guard=[self.tok_x2],
                apply_wait=self.tok_ffn_T_list,
                row_wait=[self.tok_x2])
            tok_moe2_x, tok_moe2_T, tok_moe2_tq = self.moe(
                xnfn, tok_xn3, tok_xn3,
                tok_arena_free=[self.tok_x2], tag=1,
                tok_xres=self.tok_x2, cached=True, tok_cache=self.tok_wc_moe)
            tok_x_final = tok_moe2_x
            tok_out_psum_guard = [tok_moe2_T, tok_moe2_x]
            self.phase_out(tok_x_final, tok_psum_free=tok_out_psum_guard,
                           tok_per_tq=None)
        else:
            self.phase_out(tok_x_final, tok_psum_free=tok_out_psum_guard)

        self.emit(block, sems)
        es.close()
        return nc


def build_nc():
    return Kern().build()


_WIDE_CONST = None


def _consts():
    global _WIDE_CONST
    if _WIDE_CONST is None:
        pcol = np.arange(128, dtype=np.float32)[:, None]
        j = np.arange(WIDE, dtype=np.float32)[None, :]
        _WIDE_CONST = np.abs(pcol - (j - 1920.0)).astype(np.float32)
    return {
        "identf": np.eye(128, dtype=np.float32),
        "onesr": np.ones((1, 128), dtype=np.float32),
        "onesrb": np.ones((1, 128), dtype=ml_dtypes.bfloat16),
        "onescb": np.ones((128, 1), dtype=ml_dtypes.bfloat16),
        "widea": _WIDE_CONST,
    }


def kernel(**inputs):
    n = 8
    toks = np.ascontiguousarray(inputs["L0_attn_tokens"])
    consts = _consts()

    def r(name):
        return np.ascontiguousarray(inputs[name])

    shared = {
        "Wq": r("L0_attn_Wq"), "bq": r("L0_attn_bq").reshape(1, D),
        "Wk": r("L0_attn_Wk"), "bk": r("L0_attn_bk").reshape(1, D),
        "Wv": r("L0_attn_Wv"), "bv": r("L0_attn_bv").reshape(1, D),
        "Wo": r("L0_attn_Wo"), "bo": r("L0_attn_bo").reshape(1, D),
        "dsc0": r("L0_attn_dist_scale").reshape(1, 1),
        "W1a": r("L0_attn_W1"), "b1a": r("L0_attn_b1").reshape(1, FF),
        "W2a": r("L0_attn_W2"), "b2a": r("L0_attn_b2").reshape(1, D),
        "g1": r("L0_attn_ln1_g").reshape(1, D), "be1": r("L0_attn_ln1_b").reshape(1, D),
        "g2": r("L0_attn_ln2_g").reshape(1, D), "be2": r("L0_attn_ln2_b").reshape(1, D),
        "Ws1": r("L0_ffn_Ws1"), "Ws2": r("L0_ffn_Ws2"),
        "W1e": r("L0_ffn_W1"), "W2e": r("L0_ffn_W2"),
        "Wr": r("L0_ffn_Wr"),
        **consts,
    }
    nc = build_nc()
    in_maps = []
    for c in range(n):
        m = dict(shared)
        m["x0"] = np.ascontiguousarray(toks[c])
        in_maps.append(m)
    res = run_bass_kernel_spmd(nc, in_maps, core_ids=list(range(n)))
    out = np.stack([res.results[c]["out"] for c in range(n)], axis=0)
    if DEBUG:
        kernel._dbg = res.results
    return out


# version 09
# speedup vs baseline: 1.3582x; 1.0034x over previous
import numpy as np
import ml_dtypes
import concourse.bass as bass
import concourse.mybir as mybir
from concourse.bass_utils import run_bass_kernel_spmd

F32 = mybir.dt.float32
BF16 = mybir.dt.bfloat16
AF = mybir.ActivationFunctionType
ALU = mybir.AluOpType
AX = mybir.AxisListType

T = 2048
D = 1024
NDC = 8
H = 16
DH = 64
NE = 8
FE = 512
FF = 4096
NTC = 16
NTQ = 4
WIDE = 3968
LN_EPS = 1e-5
JUNK = 1

STAGE = 3
DEBUG = False

ENGINES = ("sync", "tensor", "vector", "scalar", "gpsimd")
DMA_SEM = {"sync": "ds", "gpsimd": "dg", "scalar": "dsc"}


class Kern:
    def __init__(self):
        self.nc = bass.Bass("TRN2", target_bir_lowering=False)
        self.steps = []
        self.ctr = {}
        self.dbg_names = []

    def dbg(self, ap, wait=()):
        if not DEBUG:
            return None
        name = "dbg%d" % len(self.dbg_names)
        d = self.nc.dram_tensor(name, list(ap.shape), ap.dtype,
                                kind="ExternalOutput").ap()
        self.dbg_names.append(name)
        return self.dma("sync", d, ap, wait=wait)

    def on(self, eng, fn, wait=(), inc=None, inc_amt=1):
        waits = tuple(w for w in wait if w is not None)
        tok = None
        if inc is not None:
            self.ctr[inc] = self.ctr.get(inc, 0) + inc_amt
            tok = (inc, self.ctr[inc])
        self.steps.append((eng, fn, waits, (inc, inc_amt) if inc else None))
        return tok

    def dma(self, eng, out, in_, wait=(), slow=False):
        return self.on(eng, lambda e, o=out, i=in_: e.dma_start(
            o, i, allow_slow_non_contiguous=slow),
            wait=wait, inc=DMA_SEM[eng], inc_amt=16)

    def emit(self, block, sems):
        for s in ("ds", "dg", "dsc"):
            if self.ctr.get(s, 0) > 0:
                v = self.ctr[s]
                self.steps.append(
                    ("sync", lambda e: e.nop(), ((s, v),), None))
        seen = {e: {} for e in ENGINES}

        def mk(eng_name):
            def body(eng):
                for (e, fn, waits, inc) in self.steps:
                    if e != eng_name:
                        continue
                    for (s, v) in waits:
                        if seen[eng_name].get(s, 0) < v:
                            eng.wait_ge(sems[s], v)
                            seen[eng_name][s] = v
                    ins = fn(eng)
                    if inc is not None:
                        ins.then_inc(sems[inc[0]], inc[1])
            return body

        for name in ENGINES:
            getattr(block, name)(mk(name))

    def declare(self):
        nc = self.nc

        def din(name, shape, dtype=F32):
            return nc.dram_tensor(name, shape, dtype, kind="ExternalInput").ap()

        self.x0_d = din("x0", [T, D])
        self.Wq_d = din("Wq", [D, D]); self.bq_d = din("bq", [1, D])
        self.Wk_d = din("Wk", [D, D]); self.bk_d = din("bk", [1, D])
        self.Wv_d = din("Wv", [D, D]); self.bv_d = din("bv", [1, D])
        self.Wo_d = din("Wo", [D, D]); self.bo_d = din("bo", [1, D])
        self.ds_d = din("dsc0", [1, 1])
        self.W1a_d = din("W1a", [D, FF]); self.b1a_d = din("b1a", [1, FF])
        self.W2a_d = din("W2a", [FF, D]); self.b2a_d = din("b2a", [1, D])
        self.g1_d = din("g1", [1, D]); self.be1_d = din("be1", [1, D])
        self.g2_d = din("g2", [1, D]); self.be2_d = din("be2", [1, D])
        self.Ws1_d = din("Ws1", [D, FE]); self.Ws2_d = din("Ws2", [FE, D])
        self.W1e_d = din("W1e", [NE, D, FE]); self.W2e_d = din("W2e", [NE, FE, D])
        self.Wr_d = din("Wr", [D, NE])
        self.identf_d = din("identf", [128, 128])
        self.onesr_d = din("onesr", [1, 128])
        self.onesrb_d = din("onesrb", [1, 128], BF16)
        self.onescb_d = din("onescb", [128, 1], BF16)
        self.widea_d = din("widea", [128, WIDE])

        self.wc_W1a_d = nc.dram_tensor("wcW1a", [D, FF], BF16, kind="Internal").ap()
        self.wc_W2a_d = nc.dram_tensor("wcW2a", [FF, D], BF16, kind="Internal").ap()
        self.wc_W1e_d = nc.dram_tensor("wcW1e", [NE, D, FE], BF16, kind="Internal").ap()
        self.wc_W2e_d = nc.dram_tensor("wcW2e", [NE, FE, D], BF16, kind="Internal").ap()
        self.wc_Ws1_d = nc.dram_tensor("wcWs1", [D, FE], BF16, kind="Internal").ap()
        self.wc_Ws2_d = nc.dram_tensor("wcWs2", [FE, D], BF16, kind="Internal").ap()
        self.out_d = nc.dram_tensor("out", [T, D], F32, kind="ExternalOutput").ap()
        self.x1sp_d = nc.dram_tensor("x1sp", [128, NDC * T], F32, kind="Internal").ap()
        self.rowsp_d = nc.dram_tensor("rowsp", [1, 2 * T], F32, kind="Internal").ap()
        self.probsp_d = nc.dram_tensor("probsp", [NE, T], BF16, kind="Internal").ap()

    def views(self):
        self.xr = self.AR1[:, :].bitcast(F32)
        self.pAq = [self.pA[:, 512 * i:512 * (i + 1)] for i in range(4)]
        self.pBq = [self.pB[:, 512 * i:512 * (i + 1)] for i in range(2)]
        self.ys_ring = [self.pBq[0], self.pBq[1], self.pC, self.pD]

        sm = self.AR7[:, 4096:4608].bitcast(F32)
        self.sm = sm
        self.lt = sm[:, 0:8]
        self.v1c = sm[:, 8:9]; self.v2c = sm[:, 9:10]; self.dcol = sm[:, 10:11]
        self.edc = sm[:, 11:12]; self.denc = sm[:, 12:13]
        self.rw1c = sm[:, 13:14]; self.rw2c = sm[:, 14:15]
        self.p1fc = sm[:, 15:16]; self.p2fc = sm[:, 16:17]
        self.eq1m = sm[:, 17:25]; self.ltmm = sm[:, 25:33]
        self.eq2m = sm[:, 33:41]; self.t2m = sm[:, 41:49]
        self.probm = sm[:, 49:57]
        self.bq8c = sm[:, 57:65]
        self.row_slot = self.AR7[0:1, 0:4096].bitcast(F32)
        self.prob_fm = self.AR7[0:8, 0:2048]

    def phase_in(self):
        p = self
        in_ring = [self.AR6[:, 0:2048].bitcast(F32),
                   self.AR6[:, 2048:4096].bitcast(F32)]
        tok_dma = [None] * NTC
        tok_done = [None] * NTC
        tr_evac = {}
        last_tr = None
        for tc in range(NTC):
            slot = in_ring[tc % 2]
            w = [tok_done[tc - 2]] if tc >= 2 else []
            tok_dma[tc] = p.dma("sync", slot,
                                self.x0_d[128 * tc:128 * (tc + 1), :], wait=w)
            last_v = None
            for dc in range(NDC):
                widx = tc * NDC + dc
                ps = self.pD[:, 0:128] if widx % 2 == 0 else self.pC[:, 0:128]
                last_tr = p.on(
                    "tensor",
                    lambda e, ps=ps, slot=slot, dc=dc: e.transpose(
                        ps, slot[:, 128 * dc:128 * (dc + 1)], self.identf[:, :]),
                    wait=[tok_dma[tc], self.tok_ident,
                          tr_evac.get(widx - 2)],
                    inc="t")
                last_v = p.on(
                    "vector",
                    lambda e, ps=ps, dc=dc, tc=tc: e.tensor_copy(
                        self.xr[:, 2048 * dc + 128 * tc:2048 * dc + 128 * (tc + 1)],
                        ps),
                    wait=[last_tr], inc="v")
                tr_evac[widx] = last_v
            tok_done[tc] = last_v
        self.tok_x_ready = tok_done[-1]
        self.tok_in_trans_done = last_tr

    def ln(self, X, OUT, tok_x, mub, rb, xbf_ring, tsc_ring,
           gamma_col=None, beta_col=None, spill=False, extra_wait=(),
           psum_guard=(), apply_wait=(), row_wait=()):
        p = self
        tok_cast = [None] * NDC
        tok_sq = [None] * NDC
        tok_mm = [None] * NDC
        for dc in range(NDC):
            xb = xbf_ring[dc % 2]
            txd = tok_x[dc] if isinstance(tok_x, list) else tok_x
            w = [txd] + list(extra_wait) + ([tok_mm[dc - 2]] if dc >= 2 else [])
            tok_cast[dc] = p.on("scalar",
                                lambda e, xb=xb, dc=dc: e.copy(xb, X(dc)),
                                wait=w, inc="s")
            sq = self.sq_ring[dc % 2]
            tok_sq[dc] = p.on("vector",
                              lambda e, xb=xb, sq=sq: e.tensor_mul(sq, xb, xb),
                              wait=[tok_cast[dc]], inc="v")
            last = None
            for q in range(4):
                p.on("tensor",
                     lambda e, xb=xb, q=q, dc=dc: e.matmul(
                         self.pAq[q][0:1, :], self.onescb[:, :],
                         xb[:, 512 * q:512 * (q + 1)],
                         start=(dc == 0), stop=(dc == NDC - 1)),
                     wait=[tok_cast[dc]] + (list(psum_guard) if dc == 0 else []),
                     inc=None)
            s2ps = [self.pBq[0], self.pBq[1], self.pC, self.pD]
            for q in range(4):
                last = p.on("tensor",
                            lambda e, sq=sq, q=q, dc=dc, s2=s2ps[q]: e.matmul(
                                s2[0:1, :], self.onescb[:, :],
                                sq[:, 512 * q:512 * (q + 1)],
                                start=(dc == 0), stop=(dc == NDC - 1)),
                            wait=[tok_sq[dc]],
                            inc="t" if q == 3 else None)
            tok_mm[dc] = last
        tok_sums = tok_mm[NDC - 1]
        row = self.row_slot
        t1 = p.on("vector", lambda e: e.tensor_copy(
            row, self.pA[0:1, 0:2048]), wait=[tok_sums] + list(row_wait), inc="v")
        t2 = p.on("vector", lambda e: e.tensor_scalar_mul(row, row, 1.0 / D),
                  wait=[t1], inc="v")
        tok_sp_mu = None
        if spill:
            tok_sp_mu = p.dma("sync", self.rowsp_d[:, 0:T], row, wait=[t2])
        tok_bc = []
        for q in range(4):
            tb = p.on("tensor", lambda e, q=q: e.matmul(
                self.pAq[q], self.onesr[:, :], row[:, 512 * q:512 * (q + 1)],
                start=True, stop=True), wait=[t2], inc="t")
            te = p.on("scalar", lambda e, q=q, mub=mub: e.copy(
                mub[:, 512 * q:512 * (q + 1)], self.pAq[q]),
                wait=[tb], inc="s")
            tok_bc.append(te)
        t3 = p.on("vector", lambda e: e.tensor_mul(
            self.pA[0:1, 0:2048], row, row), wait=[t2] + tok_bc, inc="v")
        t4 = p.on("vector", lambda e: e.tensor_copy(row[:, 0:512], self.pBq[0][0:1, :]),
                  wait=[t3, tok_sp_mu], inc="v")
        t4b = p.on("vector", lambda e: e.tensor_copy(row[:, 512:1024], self.pBq[1][0:1, :]),
                   wait=[t4], inc="v")
        t4c = p.on("vector", lambda e: e.tensor_copy(row[:, 1024:1536], self.pC[0:1, :]),
                   wait=[t4b], inc="v")
        t4d = p.on("vector", lambda e: e.tensor_copy(row[:, 1536:2048], self.pD[0:1, :]),
                   wait=[t4c], inc="v")
        t5 = p.on("vector", lambda e: e.scalar_tensor_tensor(
            row, row, 1.0 / D, self.pA[0:1, 0:2048], ALU.mult, ALU.subtract),
            wait=[t4d], inc="v")
        t6 = p.on("scalar", lambda e: e.activation(row, row, AF.Ln, bias=LN_EPS, scale=1.0),
                  wait=[t5], inc="s")
        t7 = p.on("scalar", lambda e: e.activation(row, row, AF.Exp, bias=0.0, scale=-0.5),
                  wait=[t6], inc="s")
        tok_sp_r = None
        if spill:
            tok_sp_r = p.dma("sync", self.rowsp_d[:, T:2 * T], row, wait=[t7])
        tok_bc2 = []
        for q in range(4):
            tb = p.on("tensor", lambda e, q=q: e.matmul(
                self.pAq[q], self.onesr[:, :], row[:, 512 * q:512 * (q + 1)],
                start=True, stop=True), wait=[t7, t5], inc="t")
            te = p.on("scalar", lambda e, q=q, rb=rb: e.copy(
                rb[:, 512 * q:512 * (q + 1)], self.pAq[q]),
                wait=[tb], inc="s")
            tok_bc2.append(te)
        tok_tiles = tok_bc + tok_bc2 + [tok_sp_mu, tok_sp_r]
        tok_out = []
        last = None
        tg = None
        for dc in range(NDC):
            ts = tsc_ring[dc % 2]
            tg = p.on("gpsimd", lambda e, ts=ts, dc=dc, mub=mub: e.tensor_sub(
                ts, X(dc), mub),
                wait=tok_tiles + list(apply_wait) +
                     [tok_out[-2] if len(tok_out) >= 2 else None],
                inc="g")
            if gamma_col is None:
                tv = p.on("vector", lambda e, ts=ts, dc=dc, rb=rb: e.tensor_mul(
                    OUT(dc), ts, rb), wait=[tg], inc="v")
            else:
                u = ts
                tv0 = p.on("vector", lambda e, ts=ts, rb=rb: e.tensor_mul(
                    ts, ts, rb), wait=[tg], inc="v")
                tv = p.on("scalar", lambda e, ts=ts, dc=dc: e.activation(
                    OUT(dc), ts, AF.Identity, bias=beta_col(dc), scale=gamma_col(dc)),
                    wait=[tv0], inc="s")
            tok_out.append(tv)
            last = tv
        return last, tok_out, tg

    def moe(self, xn, tok_xn, tok_xn_all, tok_arena_free=(), tag=0,
            tok_xres=None, cached=False, tok_cache=None):
        p = self
        nc = self.nc
        w1v = [self.AR4[:, 0:4096].rearrange("p (c f) -> p c f", c=8),
               self.AR4[:, 4096:8192].rearrange("p (c f) -> p c f", c=8)]
        w2v = [self.AR4[:, 8192:12288].rearrange("p (c f) -> p c f", c=4),
               self.AR4[:, 12288:16384].rearrange("p (c f) -> p c f", c=4)]
        pb_slot = [self.AR6[:, 0:2048], self.AR6[:, 2048:4096]]
        rp_scr = self.AR6[:, 4096 + 512 * 0:4096 + 512 * 1]
        rp_scr2 = self.AR6[:, 4608:5120]
        rps = [rp_scr, rp_scr2]

        h2 = self.AR3[:, 0:16384]

        def h2s(bi, fc, tq):
            off = bi * 8192 + fc * 2048 + tq * 512
            return h2[:, off:off + 512]

        groups = [("s", 0), (1, 2), (3, 4), (5, 6), (7,)]

        weng = "sync" if cached else "gpsimd"

        def w1_dram(br):
            if cached:
                src = self.wc_Ws1_d if br == "s" else self.wc_W1e_d[br]
            else:
                src = self.Ws1_d if br == "s" else self.W1e_d[br]
            return src.rearrange("(c p) f -> p c f", p=128)

        def w2_dram(br):
            if cached:
                src = self.wc_Ws2_d if br == "s" else self.wc_W2e_d[br]
            else:
                src = self.Ws2_d if br == "s" else self.W2e_d[br]
            return src.rearrange("(c p) f -> p c f", p=128)

        tok_lg_evac = None
        tok_tr_evac = None
        tok_probm_free = None
        tok_probs = []
        for tc in range(NTC):
            last = None
            for dc in range(NDC):
                last = p.on("tensor", lambda e, dc=dc, tc=tc: e.matmul(
                    self.pD[:, 0:8],
                    xn(dc)[:, 128 * tc:128 * (tc + 1)],
                    self.wrsb[:, :].rearrange("p (c f) -> p c f", c=8)[:, dc, :],
                    start=(dc == 0), stop=(dc == NDC - 1)),
                    wait=([tok_xn_all, self.tok_wr, tok_lg_evac] if dc == 0 else []),
                    inc="t" if dc == NDC - 1 else None)
            tlg = last
            V = "vector"
            prev = [None]

            def vstep(fn, extra=()):
                prev[0] = p.on(V, fn, wait=[prev[0]] + list(extra), inc="v")
                return prev[0]

            t1 = vstep(lambda e: e.tensor_copy(self.lt, self.pD[:, 0:8]),
                       extra=[tlg])
            tok_lg_evac = t1
            vstep(lambda e: e.reduce_max(self.v1c, self.lt, axis=AX.X))
            vstep(lambda e: e.tensor_single_scalar(self.eq1m, self.lt, self.v1c, ALU.is_ge))
            vstep(lambda e: e.scalar_tensor_tensor(
                self.ltmm, self.eq1m, -1e30, self.lt, ALU.mult, ALU.add))
            vstep(lambda e: e.reduce_max(self.v2c, self.ltmm, axis=AX.X))
            tv = vstep(lambda e: e.tensor_sub(self.dcol, self.v2c, self.v1c))
            te = p.on("scalar", lambda e: e.activation(self.edc, self.dcol, AF.Exp),
                      wait=[tv], inc="s")
            vstep(lambda e: e.tensor_single_scalar(self.denc, self.edc, 1.0, ALU.add),
                  extra=[te])
            vstep(lambda e: e.reciprocal(self.rw1c, self.denc))
            vstep(lambda e: e.tensor_mul(self.rw2c, self.edc, self.rw1c))
            vstep(lambda e: e.tensor_single_scalar(self.p1fc, self.rw1c, 1.0, ALU.is_ge))
            vstep(lambda e: e.tensor_single_scalar(self.p2fc, self.rw2c, 1.0, ALU.is_ge))
            vstep(lambda e: e.tensor_single_scalar(self.eq2m, self.ltmm, self.v2c, ALU.is_ge))
            vstep(lambda e: e.tensor_single_scalar(self.t2m, self.eq2m, self.p2fc, ALU.mult))
            tpr = vstep(lambda e: e.scalar_tensor_tensor(
                self.probm, self.eq1m, self.p1fc, self.t2m, ALU.mult, ALU.add),
                extra=[tok_probm_free])
            ttr = p.on("tensor", lambda e: e.transpose(
                self.pC[0:8, 0:128], self.probm, self.identf[:, :]),
                wait=[tpr, tok_tr_evac], inc="t")
            tok_probm_free = ttr
            tev = p.on("vector", lambda e, tc=tc: e.tensor_copy(
                self.prob_fm[:, 128 * tc:128 * (tc + 1)], self.pC[0:8, 0:128]),
                wait=[ttr], inc="v")
            tok_tr_evac = tev
            tok_probs.append(tev)
        tok_prob_sp = p.dma("sync", self.probsp_d, self.prob_fm, wait=[tok_probs[-1]])
        self.dbg(self.prob_fm, wait=[tok_probs[-1]])

        tok_w1 = {}
        tok_w2 = {}
        tok_he_T = {}
        tok_ys_T = {}
        tok_h2 = {}
        tok_pb = {}
        tok_gp_last = {}
        rp_free = [None, None]
        tok_x_upd = None
        tok_x_dc = [None] * NDC
        un = [0]
        yn = [0]
        prev_pA_free = {}
        ys_free = [None] * 4

        for gi, g in enumerate(groups):
            for j, br in enumerate(g):
                w = [tok_he_T.get(gi - 1)]
                if gi == 0:
                    w = list(tok_arena_free)
                tok_w1[(gi, j)] = p.dma(
                    weng, w1v[j][:, :, :], w1_dram(br),
                    wait=list(w) + [tok_cache])
            for j, br in enumerate(g):
                if br != "s":
                    tok_pb[(gi, j)] = p.dma(
                        "sync", pb_slot[j],
                        self.probsp_d[br:br + 1, :].broadcast_to([128, T]),
                        wait=[tok_prob_sp, tok_gp_last.get(gi - 1)])
            for j, br in enumerate(g):
                for tq in range(NTQ):
                    for fcp in range(2):
                        u = un[0]; un[0] += 1
                        pp = [self.pAq[2 * (u % 2)], self.pAq[2 * (u % 2) + 1]]
                        lastmm = None
                        for dc in range(NDC):
                            for f2 in range(2):
                                fc = fcp * 2 + f2
                                lastmm = p.on("tensor", lambda e, j=j, dc=dc, fc=fc, tq=tq, ps=pp[f2]: e.matmul(
                                    ps, w1v[j][:, dc, 128 * fc:128 * (fc + 1)],
                                    xn(dc)[:, 512 * tq:512 * (tq + 1)],
                                    start=(dc == 0), stop=(dc == NDC - 1)),
                                    wait=([tok_w1[(gi, j)], tok_xn_all,
                                           prev_pA_free.get(u % 2)] if dc == 0 and f2 == 0 else []),
                                    inc="t" if (dc == NDC - 1 and f2 == 1) else None)
                        tok_he_T[gi] = lastmm
                        last_ev = None
                        for f2 in range(2):
                            fc = fcp * 2 + f2
                            re = h2s(j, fc, tq)
                            tre = p.on("scalar", lambda e, re=re, ps=pp[f2]: e.activation(
                                re, ps, AF.Relu), wait=[lastmm], inc="s")
                            last_ev = tre
                            if br == "s":
                                th = p.on("vector", lambda e, re=re: e.tensor_mul(
                                    re, re, re), wait=[tre], inc="v")
                            else:
                                rp = rps[f2]
                                tg_ = p.on("vector", lambda e, re=re, rp=rp, j=j, tq=tq: e.tensor_mul(
                                    rp, re, pb_slot[j][:, 512 * tq:512 * (tq + 1)]),
                                    wait=[tre, tok_pb[(gi, j)], rp_free[f2]], inc="v")
                                tok_gp_last[gi] = tg_
                                th = p.on("vector", lambda e, re=re, rp=rp: e.tensor_mul(
                                    re, rp, re), wait=[tg_], inc="v")
                                rp_free[f2] = th
                            tok_h2[(gi, tq)] = th
                        prev_pA_free[u % 2] = last_ev
            for j, br in enumerate(g):
                tok_w2[(gi, j)] = p.dma(
                    weng, w2v[j][:, :, :], w2_dram(br),
                    wait=[tok_ys_T.get(gi - 1), tok_cache] +
                         (list(tok_arena_free) if gi == 0 else []))
            for do in range(NDC):
                for tq in range(NTQ):
                    k = yn[0] % 4; yn[0] += 1
                    ps = self.ys_ring[k]
                    lastmm = None
                    nbr = len(g)
                    for j in range(nbr):
                        for fc in range(4):
                            first = (j == 0 and fc == 0)
                            lastmm = p.on("tensor", lambda e, j=j, fc=fc, do=do, tq=tq, ps=ps: e.matmul(
                                ps, w2v[j][:, fc, 128 * do:128 * (do + 1)],
                                h2s(j, fc, tq),
                                start=(j == 0 and fc == 0), stop=(j == nbr - 1 and fc == 3)),
                                wait=([tok_w2[(gi, 0)], tok_w2[(gi, nbr - 1)],
                                       tok_h2[(gi, tq)], ys_free[k]] if first else []),
                                inc="t" if (j == nbr - 1 and fc == 3) else None)
                    tok_ys_T[gi] = lastmm
                    tok_x_upd = p.on("vector", lambda e, do=do, tq=tq, ps=ps: e.scalar_tensor_tensor(
                        self.xr[:, 2048 * do + 512 * tq:2048 * do + 512 * (tq + 1)],
                        ps, 0.5,
                        self.xr[:, 2048 * do + 512 * tq:2048 * do + 512 * (tq + 1)],
                        ALU.mult, ALU.add),
                        wait=[lastmm, tok_xres], inc="v")
                    ys_free[k] = tok_x_upd
                    if gi == len(groups) - 1 and tq == NTQ - 1:
                        tok_x_dc[do] = tok_x_upd
        return tok_x_upd, tok_ys_T[len(groups) - 1], tok_x_dc

    def qkv_phase(self, xn, tok_xn_all, tok_x_spill, tok_gp_done):
        p = self
        wslot = [self.AR4[:, 0:8192].rearrange("p (c f) -> p c f", c=8),
                 self.AR4[:, 8192:16384].rearrange("p (c f) -> p c f", c=8)]
        qv = self.AR1[:, 0:16384]
        kv = self.AR1[:, 16384:32768]
        tok_bq8 = p.on("vector", lambda e: e.tensor_scalar_mul(
            self.bq8c, self.bq8c, 0.125), wait=[self.tok_bq], inc="v")
        bvb = self.AR5[:, 0:2048].bitcast(F32)
        tok_bvb = p.dma("sync", bvb, self.bv_d.broadcast_to([128, D]),
                        wait=[self.tok_bvb_free])
        tok_wq = p.dma("gpsimd", wslot[0][:, :, :],
                       self.Wq_d.rearrange("(c p) f -> p c f", p=128),
                       wait=[self.tok_w_arena_free])
        tok_wk = p.dma("gpsimd", wslot[1][:, :, :],
                       self.Wk_d.rearrange("(c p) f -> p c f", p=128),
                       wait=[self.tok_w_arena_free])
        ring = [self.pAq[0], self.pAq[1], self.pAq[2], self.pAq[3]]
        rfree = [self.tok_psum_free] * 4
        un = [0]

        def proj(wv_, tok_w, dest, bias_col, scale, extra_w=()):
            toks = []
            for do in range(NDC):
                for tp in range(NTQ):
                    k = un[0] % 4; un[0] += 1
                    ps = ring[k]
                    lastmm = None
                    for dc in range(NDC):
                        lastmm = p.on("tensor", lambda e, dc=dc, do=do, tp=tp, ps=ps, wv_=wv_: e.matmul(
                            ps, wv_[:, dc, 128 * do:128 * (do + 1)],
                            xn(dc)[:, 512 * tp:512 * (tp + 1)],
                            start=(dc == 0), stop=(dc == NDC - 1)),
                            wait=([tok_w, tok_xn_all, rfree[k]] + list(extra_w)
                                  if dc == 0 else []),
                            inc="t" if dc == NDC - 1 else None)
                    tev = p.on("scalar", lambda e, ps=ps, do=do, tp=tp, dest=dest,
                               bias_col=bias_col, scale=scale: e.activation(
                                   dest[:, 2048 * do + 512 * tp:2048 * do + 512 * (tp + 1)],
                                   ps, AF.Identity, bias=bias_col(do), scale=scale),
                               wait=[lastmm], inc="s")
                    rfree[k] = tev
                    toks.append(tev)
            return toks, lastmm

        qtoks, tok_q_T = proj(wslot[0], tok_wq, qv,
                              lambda do: self.bq8c[:, do:do + 1], 0.125,
                              extra_w=[tok_x_spill, tok_gp_done, tok_bq8])
        ktoks, tok_k_T = proj(wslot[1], tok_wk, kv,
                              lambda do: self.pcols[:, 0 + do:1 + do], 1.0,
                              extra_w=[tok_x_spill, tok_gp_done, self.tok_pcols])
        tok_wv = p.dma("gpsimd", wslot[0][:, :, :],
                       self.Wv_d.rearrange("(c p) f -> p c f", p=128),
                       wait=[tok_q_T])
        vaug = self.AR3[:, 0:16640]
        tok_v = []
        vfree = {0: self.tok_psum_free, 1: self.tok_psum_free}
        for tc in range(NTC):
            ps = [self.pBq[0], self.pBq[1]] if tc % 2 == 0 else [self.pC, self.pD]
            lastmm = None
            for dc in range(NDC):
                for hh in range(2):
                    lastmm = p.on("tensor", lambda e, dc=dc, tc=tc, hh=hh, ps=ps: e.matmul(
                        ps[hh], xn(dc)[:, 128 * tc:128 * (tc + 1)],
                        wslot[0][:, dc, 512 * hh:512 * (hh + 1)],
                        start=(dc == 0), stop=(dc == NDC - 1)),
                        wait=([tok_wv, vfree[tc % 2]] if dc == 0 and hh == 0 else []),
                        inc="t" if (dc == NDC - 1 and hh == 1) else None)
            dst = vaug[:, 1040 * tc:1040 * (tc + 1)].rearrange("p (h x) -> p h x", h=16)
            tvv = None
            for hh in range(2):
                src = ps[hh].rearrange("p (h x) -> p h x", h=8)
                bvv = bvb[:, 512 * hh:512 * (hh + 1)].rearrange("p (h x) -> p h x", h=8)
                tvv = p.on("vector", lambda e, src=src, dst=dst, hh=hh, bvv=bvv: e.tensor_tensor(
                    dst[:, 8 * hh:8 * (hh + 1), 0:64], src, bvv, ALU.add),
                    wait=[lastmm, tok_bvb, self.tok_varena_free if tc == 0 and hh == 0 else None],
                    inc="v")
            tm = p.on("vector", lambda e, dst=dst: e.memset(dst[:, :, 64:65], 1.0),
                      wait=[tvv], inc="v")
            vfree[tc % 2] = tm
            tok_v.append(tm)
        self.tok_qkv = (qtoks[-1], ktoks[-1], tok_v[-1])
        self.tok_qkv_T_last = lastmm
        return qtoks[-1], ktoks[-1], tok_v[-1]

    def attn_phase(self, tok_q, tok_k, tok_v):
        p = self
        kv_off = 16384
        wide = self.AR6[:, 0:7936].bitcast(F32)
        widb = self.AR6[:, 0:3968]
        t_wa = p.dma("sync", wide, self.widea_d, wait=[self.tok_wide_arena_free])
        acol = self.sm[:, 96:97]
        t_ds = p.dma("sync", acol, self.ds_d.broadcast_to([128, 1]))
        t_a = p.on("vector", lambda e: e.tensor_scalar_mul(acol, acol, -0.125),
                   wait=[t_ds], inc="v")
        t_b = p.on("vector", lambda e: e.tensor_single_scalar(
            wide, wide, acol, ALU.mult), wait=[t_wa, t_a], inc="v")
        mbcol = self.sm[:, 100:101]
        t_mb = p.on("vector", lambda e: e.reduce_max(mbcol, wide, axis=AX.X),
                    wait=[t_b], inc="v")
        t_tr = p.on("tensor", lambda e: e.transpose(
            self.pD[0:1, 0:128], mbcol, self.identf[:, :]),
            wait=[t_mb, self.tok_psum_free], inc="t")
        mbrow = self.row_slot[:, 0:128]
        t_cp = p.on("vector", lambda e: e.tensor_copy(mbrow, self.pD[0:1, 0:128]),
                    wait=[t_tr], inc="v")
        mbs = self.sm[0:1, 104:105]
        t_rd = p.on("vector", lambda e: e.reduce_max(mbs, mbrow, axis=AX.X,
                                                     negate=True),
                    wait=[t_cp], inc="v")
        t_bc = p.on("tensor", lambda e: e.matmul(
            self.pD[:, 0:1], self.onesr[:, :], mbs, start=True, stop=True),
            wait=[t_rd], inc="t")
        nmb = self.sm[:, 108:109]
        t_nmb = p.on("vector", lambda e: e.tensor_copy(nmb, self.pD[:, 0:1]),
                     wait=[t_bc], inc="v")
        t_dp = p.on("scalar", lambda e: e.activation(widb, wide, AF.Exp,
                                                     bias=nmb, scale=1.0),
                    wait=[t_nmb], inc="s")
        self.tok_dprime = t_dp

        slots = [self.pA[:, 0:1024], self.pA[:, 1024:2048], self.pB[:, 0:1024]]
        accs = [self.pB[:, 1024:1536], self.pB[:, 1536:2048]]
        ptb = [self.AR5[:, 1024 * k:1024 * (k + 1)] for k in range(3)]
        ebuf = [self.AR5[:, 3072 + 1024 * k:3072 + 1024 * (k + 1)] for k in range(3)]
        linv = self.AR5[:, 6144:7168].bitcast(F32)
        lrow = self.row_slot[:, 0:512]

        def head_ap(base, h, lo, sz):
            po = 64 * (h % 2)
            dc = h // 2
            return self.AR1[po:po + 64, base + 2048 * dc + lo:base + 2048 * dc + lo + sz]

        slot_free = [self.tok_psum_free] * 3
        e_free = [None] * 3
        pt_free = [None] * 3
        acc_free = [self.tok_psum_free, t_nmb]
        tok_a_evac = None
        tok_last_T = None
        LAG = 2

        for pair in range(8):
            hA, hB = 2 * pair, 2 * pair + 1
            for tb in range(4):
                tok_pv = None
                tv_done = [None] * NTC

                def emit_pv(uc):
                    last = None
                    for hh, h in enumerate((hA, hB)):
                        last = p.on("tensor", lambda e, uc=uc, hh=hh, h=h: e.matmul(
                            accs[hh][0:65, :],
                            self.av_slice(uc, h),
                            ptb[uc % 3][:, 512 * hh:512 * (hh + 1)],
                            start=(uc == 0), stop=(uc == NTC - 1)),
                            wait=[tv_done[uc], tok_v] +
                                 ([acc_free[hh]] if uc == 0 else []),
                            inc="t" if hh == 1 else None)
                    return last

                for uc in range(NTC):
                    sl = uc % 3
                    po = slots[sl]
                    lastmm = None
                    for hh, h in enumerate((hA, hB)):
                        lastmm = p.on("tensor", lambda e, h=h, po=po, hh=hh, uc=uc, tb=tb: e.matmul(
                            po[:, 512 * hh:512 * (hh + 1)],
                            head_ap(kv_off, h, 128 * uc, 128),
                            head_ap(0, h, 512 * tb, 512),
                            start=True, stop=True,
                            tile_position=(64 * (h % 2), 0)),
                            wait=([slot_free[sl], tok_q, tok_k, self.tok_dprime]
                                  if hh == 0 else []),
                            inc="t" if hh == 1 else None)
                    te = p.on("scalar", lambda e, po=po, sl=sl: e.activation(
                        ebuf[sl], po, AF.Exp, bias=-12.0, scale=1.0),
                        wait=[lastmm, e_free[sl]], inc="s")
                    slot_free[sl] = te
                    off = 1920 + 512 * tb - 128 * uc
                    tv = None
                    for hh in range(2):
                        tv = p.on("vector", lambda e, sl=sl, off=off, hh=hh: e.tensor_tensor(
                            ptb[sl][:, 512 * hh:512 * (hh + 1)],
                            ebuf[sl][:, 512 * hh:512 * (hh + 1)],
                            widb[:, off:off + 512], ALU.mult),
                            wait=[te, pt_free[sl]] if hh == 0 else [],
                            inc="v" if hh == 1 else None)
                    if uc == 0:
                        if hA > 0:
                            tv = p.on("vector", lambda e, sl=sl, hA=hA: e.memset(
                                ptb[sl][0:hA, 0:512], 0.0), wait=[tv], inc="v")
                        tv = p.on("vector", lambda e, sl=sl, hB=hB: e.memset(
                            ptb[sl][0:hB, 512:1024], 0.0), wait=[tv], inc="v")
                    tv_done[uc] = tv
                    e_free[sl] = tv
                    if uc >= 2:
                        tok_pv = emit_pv(uc - 2)
                        pt_free[(uc - 2) % 3] = tok_pv
                for uc in range(NTC - 2, NTC):
                    tok_pv = emit_pv(uc)
                    pt_free[uc % 3] = tok_pv
                tok_last_T = tok_pv
                for hh, h in enumerate((hA, hB)):
                    ps = accs[hh]
                    t1 = p.on("vector", lambda e, ps=ps: e.tensor_single_scalar(
                        lrow, ps[64:65, :], 1e-30, ALU.add),
                        wait=[tok_pv], inc="v")
                    t2a = p.on("scalar", lambda e: e.activation(
                        lrow, lrow, AF.Ln), wait=[t1], inc="s")
                    t2 = p.on("scalar", lambda e: e.activation(
                        lrow, lrow, AF.Exp, bias=0.0, scale=-1.0),
                        wait=[t2a], inc="s")
                    t3 = p.on("tensor", lambda e: e.matmul(
                        self.pA[0:64, 0:512], self.onesr[0:1, 0:64], lrow,
                        start=True, stop=True),
                        wait=[t2, slot_free[0]], inc="t")
                    t4 = p.on("vector", lambda e: e.tensor_copy(
                        linv[0:64, :], self.pA[0:64, 0:512]),
                        wait=[t3], inc="v")
                    slot_free[0] = t4
                    dc = h // 2
                    po_ = 64 * (h % 2)
                    dst = self.AR2[po_:po_ + 64,
                                   2048 * dc + 512 * tb:2048 * dc + 512 * (tb + 1)]
                    t5 = p.on("vector", lambda e, ps=ps, dst=dst: e.tensor_tensor(
                        dst, ps[0:64, :], linv[0:64, :], ALU.mult),
                        wait=[t4, self.tok_a_arena_free
                              if (pair == 0 and tb == 0) else None],
                        inc="v")
                    acc_free[hh] = t5
                    tok_a_evac = t5
        self.tok_attn_A = tok_a_evac
        self.tok_attn_T = tok_last_T
        self.tok_pv_free_final = list(acc_free) + list(acc_free)
        self.tok_pa_free_final = list(slot_free)
        return tok_a_evac

    def av_slice(self, uc, h):
        return self.AR3[:, 1040 * uc + 65 * h:1040 * uc + 65 * (h + 1)]

    def wo_ln1_phase(self):
        p = self
        wslot = self.AR4[:, 0:8192].rearrange("p (c f) -> p c f", c=8)
        tok_wo = p.dma("gpsimd", wslot[:, :, :],
                       self.Wo_d.rearrange("(c p) f -> p c f", p=128),
                       wait=[self.tok_qkv_T_last])
        mu2b = self.AR3[:, 0:4096].bitcast(F32)
        r2b = self.AR3[:, 4096:8192].bitcast(F32)
        row = self.row_slot
        tok_tiles = []
        tpsg = list(self.tok_pa_free_final) + list(self.tok_pv_free_final)
        for i, dstt in enumerate((mu2b, r2b)):
            trr = p.dma("sync", row, self.rowsp_d[:, T * i:T * (i + 1)],
                        wait=[self.tok_attn_A] + (tok_tiles[-4:] if i else []))
            for q in range(4):
                tb = p.on("tensor", lambda e, q=q: e.matmul(
                    self.pAq[q], self.onesr[:, :], row[:, 512 * q:512 * (q + 1)],
                    start=True, stop=True),
                    wait=[trr] + (tpsg if i == 0 and q == 0 else []), inc="t")
                te = p.on("scalar", lambda e, q=q, dstt=dstt: e.copy(
                    dstt[:, 512 * q:512 * (q + 1)], self.pAq[q]),
                    wait=[tb], inc="s")
                tok_tiles.append(te)
        x1r = [self.AR6[:, 0:4096].bitcast(F32), self.AR6[:, 4096:8192].bitcast(F32)]
        tsc = [self.AR5[:, 0:1024].bitcast(F32), self.AR5[:, 1024:2048].bitcast(F32)]
        usc = [self.AR5[:, 4096:5120].bitcast(F32), self.AR5[:, 5120:6144].bitcast(F32)]
        tok_x1 = [None] * NDC
        tok_R = []
        ring = [self.pAq[0], self.pAq[1], self.pAq[2], self.pAq[3]]
        rfree = [tok_tiles[-1]] * 4
        tfree = [None, None]
        ufree = [None, None]
        un = [0]
        last_wo_T = None
        for do in range(NDC):
            tok_x1[do] = p.dma(
                "sync", x1r[do % 2],
                self.x1sp_d[:, 2048 * do:2048 * (do + 1)],
                wait=[self.tok_x1_spilled, self.tok_attn_T,
                      tok_R[-5] if len(tok_R) >= 5 else None])
            for tp in range(NTQ):
                k = un[0] % 4; un[0] += 1
                ps = ring[k]
                lastmm = None
                for dc in range(NDC):
                    lastmm = p.on("tensor", lambda e, dc=dc, do=do, tp=tp, ps=ps: e.matmul(
                        ps, wslot[:, dc, 128 * do:128 * (do + 1)],
                        self.AR2[:, 2048 * dc + 512 * tp:2048 * dc + 512 * (tp + 1)],
                        start=(dc == 0), stop=(dc == NDC - 1)),
                        wait=([tok_wo, self.tok_attn_A, rfree[k]] if dc == 0 else []),
                        inc="t" if dc == NDC - 1 else None)
                last_wo_T = lastmm
                tg = p.on("gpsimd", lambda e, do=do, tp=tp: e.tensor_sub(
                    tsc[tp % 2], x1r[do % 2][:, 512 * tp:512 * (tp + 1)],
                    mu2b[:, 512 * tp:512 * (tp + 1)]),
                    wait=[tok_x1[do], tok_tiles[-1], tfree[tp % 2]], inc="g")
                tu = p.on("vector", lambda e, tp=tp: e.tensor_mul(
                    usc[tp % 2], tsc[tp % 2], r2b[:, 512 * tp:512 * (tp + 1)]),
                    wait=[tg, ufree[tp % 2]], inc="v")
                tfree[tp % 2] = tu
                tR = p.on("vector", lambda e, ps=ps, do=do, tp=tp: e.scalar_tensor_tensor(
                    self.xr[:, 2048 * do + 512 * tp:2048 * do + 512 * (tp + 1)],
                    ps, self.pcols[:, 8 + do:9 + do], usc[tp % 2],
                    ALU.add, ALU.add),
                    wait=[lastmm, tu],
                    inc="v")
                ufree[tp % 2] = tR
                rfree[k] = tR
                tok_R.append(tR)
        self.tok_R = tok_R[-1]
        self.tok_wo_T = last_wo_T
        mub = self.AR5[:, 0:4096].bitcast(F32)
        rb = self.AR5[:, 4096:8192].bitcast(F32)
        xbf_ring = [self.AR6[:, 0:2048], self.AR6[:, 2048:4096]]
        tsc_ring = [self.AR6[:, 0:4096].bitcast(F32),
                    self.AR6[:, 4096:8192].bitcast(F32)]
        xfn = lambda dc: self.xr[:, 2048 * dc:2048 * (dc + 1)]
        xafn = lambda dc: self.AR2[:, 2048 * dc:2048 * (dc + 1)]
        tok_xa, tok_xa_all, tok_gp = self.ln(
            xfn, xafn, self.tok_R, mub, rb, xbf_ring, tsc_ring,
            gamma_col=lambda dc: self.pcols[:, 56 + dc:57 + dc],
            beta_col=lambda dc: self.pcols[:, 64 + dc:65 + dc],
            extra_wait=[self.tok_R],
            psum_guard=[self.tok_R],
            apply_wait=[last_wo_T],
            row_wait=[tok_tiles[-1]])
        self.tok_xa = tok_xa
        self.tok_ln1_gp = tok_gp
        return tok_xa

    def ffn_ln2_phase(self):
        p = self
        w1s = [self.AR4[:, 0:4096].rearrange("p (c f) -> p c f", c=8),
               self.AR4[:, 4096:8192].rearrange("p (c f) -> p c f", c=8)]
        w2s = [self.AR4[:, 8192:12288].rearrange("p (c f) -> p c f", c=32),
               self.AR4[:, 12288:16384].rearrange("p (c f) -> p c f", c=32)]
        h2 = self.AR3[:, 0:16384]
        R2 = self.AR5[:, 0:8192].bitcast(F32)
        xbf = [self.AR6[:, 0:512], self.AR6[:, 512:1024]]
        sqr = [self.AR6[:, 1024:1536], self.AR6[:, 1536:2048]]
        x1r = [self.AR6[:, 2048:3072].bitcast(F32), self.AR6[:, 3072:4096].bitcast(F32)]
        zsc = [self.AR6[:, 4096:5120].bitcast(F32), self.AR6[:, 5120:6144].bitcast(F32)]
        tscr = [self.AR6[:, 6144:7168].bitcast(F32), self.AR6[:, 7168:8192].bitcast(F32)]
        mub2 = self.AR7[:, 0:1024].bitcast(F32)
        rb2 = self.AR7[:, 1024:2048].bitcast(F32)
        rowq = self.AR7[0:1, 2048:3072].bitcast(F32)

        tok_w1 = [None, None]
        tok_w2 = [None, None]
        w1_free = [self.tok_wo_T, self.tok_wo_T]
        w2_free = [self.tok_wo_T, self.tok_wo_T]
        pa_free = {0: self.tok_xa, 1: self.tok_xa}
        ys_free = [self.tok_xa] * 4
        r2_free = None
        tok_x2 = None
        xa = lambda dc: self.AR2[:, 2048 * dc:2048 * (dc + 1)]
        un = [0]
        x1_free = [self.tok_xa, self.tok_xa]
        tok_ln2_prev = None
        self.tok_x2_dc = [None] * NDC
        for tq in range(NTQ):
            tok_h2 = None
            for fg in range(8):
                j = fg % 2
                tok_w1[j] = p.dma("sync", w1s[j][:, :, :],
                                  self.wc_W1a_d[:, 512 * fg:512 * (fg + 1)].rearrange(
                                      "(c p) f -> p c f", p=128),
                                  wait=[w1_free[j], self.tok_wc_ffn])
                for fcp in range(2):
                    u = un[0] % 2; un[0] += 1
                    pp = [self.pAq[2 * u], self.pAq[2 * u + 1]]
                    lastmm = None
                    for dc in range(NDC):
                        for f2 in range(2):
                            fc = fcp * 2 + f2
                            lastmm = p.on("tensor", lambda e, j=j, dc=dc, fc=fc, tq=tq, ps=pp[f2]: e.matmul(
                                ps, w1s[j][:, dc, 128 * fc:128 * (fc + 1)],
                                xa(dc)[:, 512 * tq:512 * (tq + 1)],
                                start=(dc == 0), stop=(dc == NDC - 1)),
                                wait=([tok_w1[j], self.tok_xa, pa_free[u]]
                                      if dc == 0 and f2 == 0 else []),
                                inc="t" if (dc == NDC - 1 and f2 == 1) else None)
                    w1_free[j] = lastmm
                    last_ev = None
                    for f2 in range(2):
                        fc = fcp * 2 + f2
                        gfc = fg * 4 + fc
                        last_ev = p.on("scalar", lambda e, gfc=gfc, ps=pp[f2]: e.activation(
                            h2[:, 512 * gfc:512 * (gfc + 1)], ps, AF.Relu,
                            bias=self.pcols[:, 16 + gfc:17 + gfc], scale=1.0),
                            wait=[lastmm, self.tok_pcols,
                                  self.tok_h2_dead if tq == 0 and gfc == 0 else None],
                            inc="s")
                        tok_h2 = last_ev
                    pa_free[u] = last_ev
            tok_R2 = []
            for do in range(NDC):
                j = do % 2
                tok_w2[j] = p.dma("sync", w2s[j][:, :, :],
                                  self.wc_W2a_d[:, 128 * do:128 * (do + 1)].rearrange(
                                      "(c p) f -> p c f", p=128),
                                  wait=[w2_free[j], self.tok_wc_ffn])
                k = (do + 4 * tq) % 4
                ps = self.ys_ring[k]
                lastmm = None
                for fc in range(32):
                    lastmm = p.on("tensor", lambda e, fc=fc, do=do, j=j, ps=ps: e.matmul(
                        ps, w2s[j][:, fc, :],
                        h2[:, 512 * fc:512 * (fc + 1)],
                        start=(fc == 0), stop=(fc == 31)),
                        wait=([tok_w2[j], tok_h2, ys_free[k]] if fc == 0 else []),
                        inc="t" if fc == 31 else None)
                w2_free[j] = lastmm
                tR2 = p.on("vector", lambda e, ps=ps, do=do, tq=tq: e.scalar_tensor_tensor(
                    R2[:, 512 * do:512 * (do + 1)], ps,
                    self.pcols[:, 48 + do:49 + do],
                    xa(do)[:, 512 * tq:512 * (tq + 1)], ALU.add, ALU.add),
                    wait=[lastmm, r2_free, self.tok_pcols], inc="v")
                ys_free[k] = tR2
                tok_R2.append(tR2)
            tok_cast = [None] * NDC
            tok_mm = [None] * NDC
            for dc in range(NDC):
                xb = xbf[dc % 2]
                tok_cast[dc] = p.on("scalar", lambda e, xb=xb, dc=dc: e.copy(
                    xb, R2[:, 512 * dc:512 * (dc + 1)]),
                    wait=[tok_R2[dc], tok_mm[dc - 2] if dc >= 2 else None], inc="s")
                sq = sqr[dc % 2]
                tsq = p.on("vector", lambda e, xb=xb, sq=sq: e.tensor_mul(sq, xb, xb),
                           wait=[tok_cast[dc]], inc="v")
                p.on("tensor", lambda e, xb=xb, dc=dc: e.matmul(
                    self.pAq[0][0:1, :], self.onescb[:, :], xb,
                    start=(dc == 0), stop=(dc == NDC - 1)),
                    wait=[tok_cast[dc]] + ([pa_free[0], pa_free[1]] if dc == 0 else []))
                tok_mm[dc] = p.on("tensor", lambda e, sq=sq, dc=dc: e.matmul(
                    self.pAq[1][0:1, :], self.onescb[:, :], sq,
                    start=(dc == 0), stop=(dc == NDC - 1)),
                    wait=[tsq], inc="t")
            tok_sums = tok_mm[NDC - 1]
            rowq2 = self.AR7[0:1, 3072:4096].bitcast(F32)
            t1 = p.on("vector", lambda e: e.tensor_copy(rowq, self.pAq[0][0:1, :]),
                      wait=[tok_sums], inc="v")
            t1b = p.on("vector", lambda e, rowq2=rowq2: e.tensor_copy(
                rowq2, self.pAq[1][0:1, :]), wait=[t1], inc="v")
            t2 = p.on("vector", lambda e: e.tensor_scalar_mul(rowq, rowq, 1.0 / D),
                      wait=[t1b], inc="v")
            tb1 = p.on("tensor", lambda e: e.matmul(
                self.pAq[0], self.onesr[:, :], rowq, start=True, stop=True),
                wait=[t2], inc="t")
            te1 = p.on("scalar", lambda e: e.copy(mub2[:, 0:512], self.pAq[0]),
                       wait=[tb1], inc="s")
            t3 = p.on("vector", lambda e: e.tensor_mul(rowq, rowq, rowq),
                      wait=[tb1], inc="v")
            t5 = p.on("vector", lambda e, rowq2=rowq2: e.scalar_tensor_tensor(
                rowq2, rowq2, 1.0 / D, rowq, ALU.mult, ALU.subtract),
                wait=[t3], inc="v")
            t6 = p.on("scalar", lambda e, rowq2=rowq2: e.activation(
                rowq2, rowq2, AF.Ln, bias=LN_EPS, scale=1.0),
                wait=[t5], inc="s")
            t7 = p.on("scalar", lambda e, rowq2=rowq2: e.activation(
                rowq2, rowq2, AF.Exp, bias=0.0, scale=-0.5),
                wait=[t6], inc="s")
            tb2 = p.on("tensor", lambda e, rowq2=rowq2: e.matmul(
                self.pAq[1], self.onesr[:, :], rowq2, start=True, stop=True),
                wait=[t7], inc="t")
            te2 = p.on("scalar", lambda e: e.copy(rb2[:, 0:512], self.pAq[1]),
                       wait=[tb2], inc="s")
            pa_free = {0: te2, 1: pa_free[1]}
            un[0] += (un[0] + 1) % 2
            tok_ln2_prev = None
            for dc in range(NDC):
                tok_x1q = p.dma("sync", x1r[dc % 2],
                                self.x1sp_d[:, 2048 * dc + 512 * tq:
                                            2048 * dc + 512 * (tq + 1)],
                                wait=[x1_free[dc % 2]])
                ts_ = tscr[dc % 2]
                tg = p.on("gpsimd", lambda e, ts_=ts_, dc=dc: e.tensor_sub(
                    ts_, R2[:, 512 * dc:512 * (dc + 1)], mub2[:, 0:512]),
                    wait=[te1, te2, tok_ln2_prev], inc="g")
                tu = p.on("vector", lambda e, ts_=ts_: e.tensor_mul(
                    ts_, ts_, rb2[:, 0:512]), wait=[tg], inc="v")
                z = zsc[dc % 2]
                tz = p.on("scalar", lambda e, ts_=ts_, z=z, dc=dc: e.activation(
                    z, ts_, AF.Identity, bias=self.pcols[:, 80 + dc:81 + dc],
                    scale=self.pcols[:, 72 + dc:73 + dc]),
                    wait=[tu], inc="s")
                tok_x2 = p.on("vector", lambda e, z=z, dc=dc, tq=tq: e.tensor_add(
                    self.xr[:, 2048 * dc + 512 * tq:2048 * dc + 512 * (tq + 1)],
                    z, x1r[dc % 2]),
                    wait=[tz, tok_x1q, self.tok_R], inc="v")
                x1_free[dc % 2] = tok_x2
                tok_ln2_prev = tok_x2
                if tq == NTQ - 1:
                    self.tok_x2_dc[dc] = tok_x2
            r2_free = tok_x2
        self.tok_x2 = tok_x2
        self.tok_ffn_T_list = [w1_free[0], w1_free[1]]
        return tok_x2

    def phase_out(self, tok_x_final, tok_psum_free=(), tok_per_tq=None):
        p = self
        oring = [self.AR6[:, 0:2048].bitcast(F32),
                 self.AR6[:, 2048:4096].bitcast(F32)]
        tok_out_dma = [None] * NTC
        prev_ev = {}
        for tc in range(NTC):
            slot = oring[tc % 2]
            last_ev = None
            tok_x = tok_per_tq[tc // 4] if tok_per_tq else tok_x_final
            for dc in range(NDC):
                widx = tc * NDC + dc
                ps = self.pD[:, 0:128] if widx % 2 == 0 else self.pC[:, 0:128]
                tt = p.on("tensor", lambda e, ps=ps, dc=dc, tc=tc: e.transpose(
                    ps, self.xr[:, 2048 * dc + 128 * tc:2048 * dc + 128 * (tc + 1)],
                    self.identf[:, :]),
                    wait=[tok_x, prev_ev.get(widx - 2),
                          tok_out_dma[tc - 2] if tc >= 2 else None] +
                         (list(tok_psum_free) if widx == 0 else []),
                    inc="t")
                last_ev = p.on("vector", lambda e, ps=ps, dc=dc, slot=slot: e.tensor_copy(
                    slot[:, 128 * dc:128 * (dc + 1)], ps),
                    wait=[tt], inc="v")
                prev_ev[widx] = last_ev
            tok_out_dma[tc] = p.dma(
                "sync", self.out_d[128 * tc:128 * (tc + 1), :], slot,
                wait=[last_ev])

    def build(self):
        nc = self.nc
        self.declare()
        import contextlib
        es = contextlib.ExitStack()
        sb = lambda name, shape, dtype: es.enter_context(nc.sbuf_tensor(name, shape, dtype))
        self.AR1 = sb("AR1", [128, 2 * NDC * T], BF16)
        self.AR2 = sb("AR2", [128, NDC * T], BF16)
        self.AR3 = sb("AR3", [128, 18432], BF16)
        self.AR4 = sb("AR4", [128, NDC * T], BF16)
        self.AR5 = sb("AR5", [128, 8192], BF16)
        self.AR6 = sb("AR6", [128, 8192], BF16)
        self.AR7 = sb("AR7", [128, 4608], BF16)
        self.identf = sb("identfs", [128, 128], F32)
        self.onesr = sb("onesrs", [1, 128], F32)
        self.onesrb = sb("onesrbs", [1, 128], BF16)
        self.onescb = sb("onescbs", [128, 1], BF16)
        self.wrsb = sb("wrsbs", [128, 64], BF16)
        self.pcols = sb("pcolss", [128, 96], F32)[:, :]
        self.pA = es.enter_context(nc.psum_tensor("pA", [128, 2048], F32))
        self.pB = es.enter_context(nc.psum_tensor("pB", [128, 2048], F32))
        self.pC = self.pB[:, 1024:1536]
        self.pD = self.pB[:, 1536:2048]
        sems = {}
        for s in ("t", "v", "s", "g", "ds", "dg", "dsc"):
            sems[s] = es.enter_context(nc.semaphore("sem_" + s))
        block = es.enter_context(nc.Block())

        self.views()

        self.tok_consts = None
        for i, cval in enumerate([LN_EPS, -0.5, 0.125, -12.0, -1.0]):
            col = self.sm[:, 66 + i:67 + i]
            self.tok_consts = self.on(
                "vector", lambda e, col=col, cval=cval: e.memset(col, cval),
                inc="v")
            nc.const_aps.register(F32, cval, col)

        self.tok_ident = self.dma("sync", self.identf[:, :], self.identf_d)
        self.tok_onesr = self.dma("sync", self.onesr[:, :], self.onesr_d)
        self.tok_onesrb = self.dma("sync", self.onesrb[:, :], self.onesrb_d)
        self.tok_onescb = self.dma("sync", self.onescb[:, :], self.onescb_d)
        self.tok_wr = self.dma("gpsimd",
                               self.wrsb[:, :].rearrange("p (c f) -> p c f", c=8),
                               self.Wr_d.rearrange("(c p) f -> p c f", p=128))
        self.tok_bq = self.dma(
            "sync", self.bq8c, self.bq_d.rearrange("o (c p) -> p (o c)", p=128),
            slow=True)
        pc = self.pcols
        tpc = None
        for dst_lo, src in ((0, self.bk_d), (8, self.bo_d), (16, self.b1a_d),
                            (48, self.b2a_d), (56, self.g1_d), (64, self.be1_d),
                            (72, self.g2_d), (80, self.be2_d)):
            w_ = src.shape[1] // 128
            tpc = self.dma("sync", pc[:, dst_lo:dst_lo + w_],
                           src.rearrange("o (c p) -> p (o c)", p=128), slow=True)
        self.tok_pcols_l = tpc

        self.phase_in()

        mub = self.AR5[:, 0:4096].bitcast(F32)
        rb = self.AR5[:, 4096:8192].bitcast(F32)
        xbf_ring = [self.AR6[:, 0:2048], self.AR6[:, 2048:4096]]
        self.sq_ring = [self.AR6[:, 4096:6144], self.AR6[:, 6144:8192]]
        tsc_ring = [self.AR6[:, 0:4096].bitcast(F32),
                    self.AR6[:, 4096:8192].bitcast(F32)]
        xfn = lambda dc: self.xr[:, 2048 * dc:2048 * (dc + 1)]
        xnfn = lambda dc: self.AR2[:, 2048 * dc:2048 * (dc + 1)]
        tok_xn_last, tok_xn_all_l, tok_ln_gp = self.ln(
            xfn, xnfn, self.tok_x_ready, mub, rb, xbf_ring, tsc_ring,
            extra_wait=[self.tok_onescb, self.tok_onesr, self.tok_in_trans_done])
        self.dbg(self.AR2[:, :], wait=[tok_xn_last])
        self.dbg(mub[:, :], wait=[tok_xn_last])
        self.dbg(rb[:, :], wait=[tok_xn_last])
        tok_moe_x, tok_moe_T, tok_moe_tq = self.moe(
            xnfn, tok_xn_last, tok_xn_last, tok_arena_free=(), tag=0,
            tok_xres=self.tok_x_ready)

        tok_x_final = tok_moe_x
        tok_out_psum_guard = [tok_moe_T]
        if STAGE >= 2:
            tok_xn2, tok_xn2_all, tok_lnb_gp = self.ln(
                xfn, xnfn, tok_moe_tq, mub, rb, xbf_ring, tsc_ring,
                spill=True,
                extra_wait=[],
                psum_guard=[tok_moe_x],
                row_wait=[])
            self.tok_x1_spilled = self.dma("sync", self.x1sp_d, self.xr,
                                           wait=[tok_moe_x])
            self.tok_w_arena_free = tok_moe_T
            self.tok_psum_free = tok_moe_x
            self.tok_varena_free = tok_moe_T
            self.tok_bvb_free = tok_lnb_gp
            self.tok_wide_arena_free = tok_xn2
            self.tok_pcols = self.tok_pcols_l
            tq_, tk_, tv_ = self.qkv_phase(
                xnfn, tok_xn2, self.tok_x1_spilled, tok_lnb_gp)
            self.tok_wc_ffn = None
            for dst, src in ((self.wc_W1a_d, self.W1a_d),
                             (self.wc_W2a_d, self.W2a_d)):
                self.tok_wc_ffn = self.dma("gpsimd", dst, src)
            self.tok_wc_moe = None
            for dst, src in ((self.wc_W1e_d, self.W1e_d),
                             (self.wc_W2e_d, self.W2e_d),
                             (self.wc_Ws1_d, self.Ws1_d),
                             (self.wc_Ws2_d, self.Ws2_d)):
                self.tok_wc_moe = self.dma("gpsimd", dst, src)
            self.tok_a_arena_free = self.tok_qkv_T_last
            self.attn_phase(tq_, tk_, tv_)
            self.wo_ln1_phase()
            self.tok_h2_dead = self.tok_R
            self.ffn_ln2_phase()
            tok_x_final = self.tok_x2
            tok_out_psum_guard = [self.tok_x2]

        if STAGE >= 3:
            tok_xn3, _, tok_lnc_gp = self.ln(
                xfn, xnfn, self.tok_x2_dc, mub, rb,
                xbf_ring, tsc_ring,
                extra_wait=[self.tok_x2],
                psum_guard=[self.tok_x2],
                apply_wait=self.tok_ffn_T_list,
                row_wait=[self.tok_x2])
            tok_moe2_x, tok_moe2_T, tok_moe2_tq = self.moe(
                xnfn, tok_xn3, tok_xn3,
                tok_arena_free=[self.tok_x2], tag=1,
                tok_xres=self.tok_x2, cached=True, tok_cache=self.tok_wc_moe)
            tok_x_final = tok_moe2_x
            tok_out_psum_guard = [tok_moe2_T, tok_moe2_x]
            self.phase_out(tok_x_final, tok_psum_free=tok_out_psum_guard,
                           tok_per_tq=None)
        else:
            self.phase_out(tok_x_final, tok_psum_free=tok_out_psum_guard)

        self.emit(block, sems)
        es.close()
        return nc


def build_nc():
    return Kern().build()


_WIDE_CONST = None


def _consts():
    global _WIDE_CONST
    if _WIDE_CONST is None:
        pcol = np.arange(128, dtype=np.float32)[:, None]
        j = np.arange(WIDE, dtype=np.float32)[None, :]
        _WIDE_CONST = np.abs(pcol - (j - 1920.0)).astype(np.float32)
    return {
        "identf": np.eye(128, dtype=np.float32),
        "onesr": np.ones((1, 128), dtype=np.float32),
        "onesrb": np.ones((1, 128), dtype=ml_dtypes.bfloat16),
        "onescb": np.ones((128, 1), dtype=ml_dtypes.bfloat16),
        "widea": _WIDE_CONST,
    }


def kernel(**inputs):
    n = 8
    toks = np.ascontiguousarray(inputs["L0_attn_tokens"])
    consts = _consts()

    def r(name):
        return np.ascontiguousarray(inputs[name])

    shared = {
        "Wq": r("L0_attn_Wq"), "bq": r("L0_attn_bq").reshape(1, D),
        "Wk": r("L0_attn_Wk"), "bk": r("L0_attn_bk").reshape(1, D),
        "Wv": r("L0_attn_Wv"), "bv": r("L0_attn_bv").reshape(1, D),
        "Wo": r("L0_attn_Wo"), "bo": r("L0_attn_bo").reshape(1, D),
        "dsc0": r("L0_attn_dist_scale").reshape(1, 1),
        "W1a": r("L0_attn_W1"), "b1a": r("L0_attn_b1").reshape(1, FF),
        "W2a": r("L0_attn_W2"), "b2a": r("L0_attn_b2").reshape(1, D),
        "g1": r("L0_attn_ln1_g").reshape(1, D), "be1": r("L0_attn_ln1_b").reshape(1, D),
        "g2": r("L0_attn_ln2_g").reshape(1, D), "be2": r("L0_attn_ln2_b").reshape(1, D),
        "Ws1": r("L0_ffn_Ws1"), "Ws2": r("L0_ffn_Ws2"),
        "W1e": r("L0_ffn_W1"), "W2e": r("L0_ffn_W2"),
        "Wr": r("L0_ffn_Wr"),
        **consts,
    }
    nc = build_nc()
    in_maps = []
    for c in range(n):
        m = dict(shared)
        m["x0"] = np.ascontiguousarray(toks[c])
        in_maps.append(m)
    res = run_bass_kernel_spmd(nc, in_maps, core_ids=list(range(n)))
    out = np.stack([res.results[c]["out"] for c in range(n)], axis=0)
    if DEBUG:
        kernel._dbg = res.results
    return out
